# version 15
# baseline (speedup 1.0000x reference)
import numpy as np
import ml_dtypes
from contextlib import ExitStack

import concourse.bass as bass
import concourse.mybir as mybir
from concourse.bass_utils import run_bass_kernel_spmd

F32 = mybir.dt.float32
BF16 = mybir.dt.bfloat16
I16 = mybir.dt.int16
U16 = mybir.dt.uint16
I32 = mybir.dt.int32
AF = mybir.ActivationFunctionType
ALU = mybir.AluOpType
AX = mybir.AxisListType

T, D, H, II = 2048, 1024, 1024, 3072
S = 128
NCH = T // S
DCH = D // 128
HCH = H // 128
ICH = II // 128
ISL = II // 8
TSUB = 512
NTS = T // TSUB
QMAX = 32767.0
TWO_PI = float(2.0 * np.pi)
INV_2PI = float(1.0 / (2.0 * np.pi))

_STAGE = "full"


def mkfence(eng, st):
    def f(instr):
        st.inc(instr)
        eng.wait_ge(st.h, st.n)
        return instr
    return f


class ST:
    def __init__(self, stack, nc, name):
        self.h = stack.enter_context(nc.semaphore(name))
        self.n = 0

    def inc(self, instr, k=1):
        instr.then_inc(self.h, k)
        self.n += k
        return instr


def _ap(x):
    return x if hasattr(x, "unsqueeze") else x[:, :]


def r3(ap, k=128):
    return _ap(ap).rearrange("p (c j) -> p c j", j=k)


def bc3(ap, reps):
    ap = _ap(ap)
    return ap.unsqueeze(2).broadcast_to([ap.shape[0], ap.shape[1], reps])


def build(stage="full"):
    nc = bass.Bass("TRN2", target_bir_lowering=False, detect_race_conditions=False)

    x_h = nc.dram_tensor("x", [T, D], F32, kind="ExternalInput")
    btre_h = nc.dram_tensor("btre", [D, H], F32, kind="ExternalInput")
    btim_h = nc.dram_tensor("btim", [D, H], F32, kind="ExternalInput")
    ctre_h = nc.dram_tensor("ctre", [H, D], F32, kind="ExternalInput")
    ctim_h = nc.dram_tensor("ctim", [H, D], F32, kind="ExternalInput")
    nucol_h = nc.dram_tensor("nucol", [128, 8], F32, kind="ExternalInput")
    thcol_h = nc.dram_tensor("thcol", [128, 8], F32, kind="ExternalInput")
    nurow_h = nc.dram_tensor("nurow", [1, H], F32, kind="ExternalInput")
    throw_h = nc.dram_tensor("throw", [1, H], F32, kind="ExternalInput")
    dskrow_h = nc.dram_tensor("dskrow", [1, D], F32, kind="ExternalInput")
    wup_h = nc.dram_tensor("wups", [ISL, D], F32, kind="ExternalInput")
    wgt_h = nc.dram_tensor("wgts", [ISL, D], F32, kind="ExternalInput")
    wdn_h = nc.dram_tensor("wdns", [D, ISL], F32, kind="ExternalInput")
    identf_h = nc.dram_tensor("identf", [128, 128], F32, kind="ExternalInput")
    identb_h = nc.dram_tensor("identb", [128, 128], BF16, kind="ExternalInput")
    lt_h = nc.dram_tensor("ltones", [128, 128], BF16, kind="ExternalInput")
    iotac_h = nc.dram_tensor("iotac", [128, 1], F32, kind="ExternalInput")
    iotacat_h = nc.dram_tensor("iotacat", [128, 1024], F32, kind="ExternalInput")
    onesr_h = nc.dram_tensor("onesr", [1, 128], F32, kind="ExternalInput")

    out_h = nc.dram_tensor("out", [T, D], F32, kind="ExternalOutput")
    dbg_h = (nc.dram_tensor("dbg", [128, 8192], F32, kind="ExternalOutput")
             if stage == "x1" else None)
    dbgs_h = (nc.dram_tensor("dbgs", [128, 16], F32, kind="ExternalOutput")
              if stage == "x1" else None)

    updram = nc.dram_tensor("updram", [II, T], F32, kind="Internal")
    gatedram = nc.dram_tensor("gatedram", [II, T], F32, kind="Internal")
    x1dram = nc.dram_tensor("x1dram", [128, NCH * D], F32, kind="Internal")
    agu_in = nc.dram_tensor("agu_in", [D, ISL], BF16, kind="Internal")
    agu_out = nc.dram_tensor("agu_out", [8 * D, ISL], BF16, kind="Internal")
    agg_in = nc.dram_tensor("agg_in", [D, ISL], BF16, kind="Internal")
    agg_out = nc.dram_tensor("agg_out", [8 * D, ISL], BF16, kind="Internal")
    agd_in = nc.dram_tensor("agd_in", [ISL, D], BF16, kind="Internal")
    agd_out = nc.dram_tensor("agd_out", [II, D], BF16, kind="Internal")
    ccin = [nc.dram_tensor(f"cc{i}_in", [1, 16], F32, kind="Internal") for i in range(4)]
    ccout = [nc.dram_tensor(f"cc{i}_out", [1, 16], F32, kind="Internal") for i in range(4)]
    RG = [list(range(8))]

    es = ExitStack()
    with es:
        e = es.enter_context

        identf = e(nc.sbuf_tensor("identf_s", [128, 128], F32))
        identb = e(nc.sbuf_tensor("identb_s", [128, 128], BF16))
        ltones = e(nc.sbuf_tensor("lt_s", [128, 128], BF16))
        iotac = e(nc.sbuf_tensor("iotac_s", [128, 1], F32))
        onesr = e(nc.sbuf_tensor("onesr_s", [1, 128], F32))
        nucol = e(nc.sbuf_tensor("nucol_s", [128, 8], F32))
        thcol = e(nc.sbuf_tensor("thcol_s", [128, 8], F32))
        nu_c = e(nc.sbuf_tensor("nu_c", [128, 8], F32))
        nneg_c = e(nc.sbuf_tensor("nneg_c", [128, 8], F32))
        th2p_c = e(nc.sbuf_tensor("th2p_c", [128, 8], F32))
        phre = e(nc.sbuf_tensor("phre", [128, 8], F32))
        phim = e(nc.sbuf_tensor("phim", [128, 8], F32))
        lcre = [e(nc.sbuf_tensor(f"lcre{i}", [128, 8], F32)) for i in range(2)]
        lcim = [e(nc.sbuf_tensor(f"lcim{i}", [128, 8], F32)) for i in range(2)]
        sm8a = e(nc.sbuf_tensor("sm8a", [128, 8], F32))
        sm8b = e(nc.sbuf_tensor("sm8b", [128, 8], F32))
        sm8c = e(nc.sbuf_tensor("sm8c", [128, 8], F32))
        sm8i = e(nc.sbuf_tensor("sm8i", [128, 8], I32))
        sums = e(nc.sbuf_tensor("sums", [128, 1], F32))
        sumsq = e(nc.sbuf_tensor("sumsq", [128, 1], F32))
        mean = e(nc.sbuf_tensor("mean", [128, 1], F32))
        varb = e(nc.sbuf_tensor("varb", [128, 1], F32))
        sdb = e(nc.sbuf_tensor("sdb", [128, 1], F32))
        rstd = e(nc.sbuf_tensor("rstd", [128, 1], F32))
        ex2 = e(nc.sbuf_tensor("ex2", [128, 1], F32))
        rmax_x1 = e(nc.sbuf_tensor("rmax_x1", [128, 1], F32))
        rmax_up = e(nc.sbuf_tensor("rmax_up", [128, 1], F32))
        rmax_gt = e(nc.sbuf_tensor("rmax_gt", [128, 1], F32))
        rmax_g2 = e(nc.sbuf_tensor("rmax_g2", [128, 1], F32))
        rmax_o = e(nc.sbuf_tensor("rmax_o", [128, 1], F32))
        tred = e(nc.sbuf_tensor("tred", [128, 1], F32))
        tredA = e(nc.sbuf_tensor("tredA", [128, 96], F32))
        tredB = e(nc.sbuf_tensor("tredB", [128, 96], F32))
        rowt = e(nc.sbuf_tensor("rowt", [1, 128], F32))
        gsc = e(nc.sbuf_tensor("gsc", [1, 16], F32))
        bcast = e(nc.sbuf_tensor("bcast", [128, 16], F32))
        s_x = e(nc.sbuf_tensor("s_x", [128, 1], F32))
        r_x = e(nc.sbuf_tensor("r_x", [128, 1], F32))
        s_up = e(nc.sbuf_tensor("s_up", [128, 1], F32))
        r_up = e(nc.sbuf_tensor("r_up", [128, 1], F32))
        s_gt = e(nc.sbuf_tensor("s_gt", [128, 1], F32))
        r_gt = e(nc.sbuf_tensor("r_gt", [128, 1], F32))
        sg216 = e(nc.sbuf_tensor("sg216", [128, 1], F32))
        s_2 = e(nc.sbuf_tensor("s_2", [128, 1], F32))
        k_2 = e(nc.sbuf_tensor("k_2", [128, 1], F32))
        s_o = e(nc.sbuf_tensor("s_o", [128, 1], F32))
        k_o = e(nc.sbuf_tensor("k_o", [128, 1], F32))
        t_c1 = e(nc.sbuf_tensor("t_c1", [128, 1], F32))

        sLD = ST(es, nc, "sLD")
        sGL = ST(es, nc, "sGL")
        sX0 = ST(es, nc, "sX0")
        sPV = ST(es, nc, "sPV")
        sPS = ST(es, nc, "sPS")
        sPT = ST(es, nc, "sPT")
        sPD = ST(es, nc, "sPD")
        sWL = ST(es, nc, "sWL")
        sAG = ST(es, nc, "sAG")
        sXT = ST(es, nc, "sXT")
        sXF = ST(es, nc, "sXF")
        sBU = ST(es, nc, "sBU")
        sZ = ST(es, nc, "sZ")
        sHm = ST(es, nc, "sHm")
        sHS = ST(es, nc, "sHS")
        sCP = ST(es, nc, "sCP")
        sY = ST(es, nc, "sY")
        sSQ = ST(es, nc, "sSQ")
        sVR = ST(es, nc, "sVR")
        sSD = ST(es, nc, "sSD")
        sLN = ST(es, nc, "sLN")
        sAR = ST(es, nc, "sAR")
        sBC = ST(es, nc, "sBC")
        sQ1 = ST(es, nc, "sQ1")
        sQ2 = ST(es, nc, "sQ2")
        sQ3 = ST(es, nc, "sQ3")
        sWS = ST(es, nc, "sWS")
        sMM = ST(es, nc, "sMM")
        sEU = ST(es, nc, "sEU")
        sEG = ST(es, nc, "sEG")
        sRM = ST(es, nc, "sRM")
        sSP = ST(es, nc, "sSP")
        sGT = ST(es, nc, "sGT")
        sSGA = ST(es, nc, "sSGA")
        sSGB = ST(es, nc, "sSGB")
        sUPL = ST(es, nc, "sUPL")
        sOC = ST(es, nc, "sOC")
        sWD = ST(es, nc, "sWD")
        sX1L = ST(es, nc, "sX1L")
        sDM = ST(es, nc, "sDM")
        sY2 = ST(es, nc, "sY2")
        sQ2s = ST(es, nc, "sQ2s")
        sV2 = ST(es, nc, "sV2")
        sD2 = ST(es, nc, "sD2")
        sLN2 = ST(es, nc, "sLN2")
        sOUT = ST(es, nc, "sOUT")
        sFV = ST(es, nc, "sFV")
        sZp = ST(es, nc, "sZp")
        sT1 = ST(es, nc, "sT1")
        sGT1 = ST(es, nc, "sGT1")
        sUPL1 = ST(es, nc, "sUPL1")
        sUP50 = ST(es, nc, "sUP50")
        sUP51 = ST(es, nc, "sUP51")
        sWS1 = ST(es, nc, "sWS1")
        sX1L1 = ST(es, nc, "sX1L1")
        sSP1 = ST(es, nc, "sSP1")
        sSPX = ST(es, nc, "sSPX")
        sGP = ST(es, nc, "sGP")
        sHSu = ST(es, nc, "sHSu")

        blk = e(nc.Block())

        _scope_stack = []

        def _scope_begin(nm):
            _scope_stack.append((nm, nc.enter_named_scope(nm, False)[0]))

        def _scope_end():
            nm, sid = _scope_stack.pop()
            nc.leave_named_scope(nm, sid, False)


        es_ab = ExitStack()
        xt = es_ab.enter_context(nc.sbuf_tensor("xt", [128, NCH * D], F32))

        es_a = ExitStack()
        ea = es_a.enter_context
        btre = ea(nc.sbuf_tensor("btre_s", [128, DCH * H], BF16))
        btim = ea(nc.sbuf_tensor("btim_s", [128, DCH * H], BF16))
        ctre = ea(nc.sbuf_tensor("ctre_s", [128, HCH * D], BF16))
        ctimn = ea(nc.sbuf_tensor("ctimn_s", [128, HCH * D], BF16))
        tnre = ea(nc.sbuf_tensor("tnre", [128, H], F32))
        tnim = ea(nc.sbuf_tensor("tnim", [128, H], F32))
        tpre = ea(nc.sbuf_tensor("tpre", [128, H], F32))
        tpim = ea(nc.sbuf_tensor("tpim", [128, H], F32))
        tp127re = ea(nc.sbuf_tensor("tp127re", [128, 8], F32))
        tp127im = ea(nc.sbuf_tensor("tp127im", [128, 8], F32))
        zre = ea(nc.sbuf_tensor("zre", [128, H], BF16))
        zim = ea(nc.sbuf_tensor("zim", [128, H], BF16))
        hsre = ea(nc.sbuf_tensor("hsre", [128, H], BF16))
        hsim = ea(nc.sbuf_tensor("hsim", [128, H], BF16))
        x0f = ea(nc.sbuf_tensor("x0f", [128, D], BF16))
        scr1 = ea(nc.sbuf_tensor("scr1", [128, 1024], F32))
        scr2 = ea(nc.sbuf_tensor("scr2", [128, 1024], F32))
        scr3 = ea(nc.sbuf_tensor("scr3", [128, 1024], F32))
        scr4 = ea(nc.sbuf_tensor("scr4", [128, 1024], F32))
        scri = ea(nc.sbuf_tensor("scri", [128, 1024], I32))
        scr5g = ea(nc.sbuf_tensor("scr5g", [128, 1024], F32))
        scr6g = ea(nc.sbuf_tensor("scr6g", [128, 1024], F32))
        scr7g = ea(nc.sbuf_tensor("scr7g", [128, 1024], F32))
        scr8g = ea(nc.sbuf_tensor("scr8g", [128, 1024], F32))
        t1g = ea(nc.sbuf_tensor("t1g", [128, 1024], F32))
        d1bc = ea(nc.sbuf_tensor("d1bc", [128, D], F32))

        es_a0 = ExitStack()
        e0 = es_a0.enter_context
        iotacat = scr7g
        nbc = scr5g
        t2pbc = scr6g
        rA = scr1
        rB = scr2
        rC = scr3
        rD = scr4
        rE = d1bc
        nurow = rA[0:1, :]
        throw = rB[0:1, :]
        dskrow = rE[0:1, :]
        nrow2 = rC[0:1, :]
        t2prow = rB[0:1, :]
        d1row = rA[0:1, :]
        grow = rD[0:1, :]
        growtmp = rA[0:1, :]
        wld = [scr1] * 2
        wqbf = zre
        wtrs = zim
        bmaxb = hsre[:, :].bitcast(F32)[:, 0:32]
        bsclb = hsre[:, :].bitcast(F32)[:, 32:64]
        binvb = hsre[:, :].bitcast(F32)[:, 64:96]

        es_pa = ExitStack()
        pA = es_pa.enter_context(nc.psum_tensor("pA", [128, 1024], F32))
        pB = es_pa.enter_context(nc.psum_tensor("pB", [128, 1024], F32))
        pHx = es_pa.enter_context(nc.psum_tensor("pHx", [128, 1024], F32))
        pC = es_pa.enter_context(nc.psum_tensor("pCc", [128, 1024], F32))
        pHr = pHx
        pHi = pC

        _scope_begin("prep")
        @blk.sync
        def _(sync):
            for dst, src in [(identf[:, :], identf_h), (identb[:, :], identb_h),
                             (ltones[:, :], lt_h), (iotac[:, :], iotac_h),
                             (onesr[:, :], onesr_h), (nucol[:, :], nucol_h),
                             (thcol[:, :], thcol_h), (nurow, nurow_h),
                             (throw, throw_h), (dskrow, dskrow_h),
                             (iotacat[:, :], iotacat_h)]:
                sLD.inc(sync.dma_start(out=dst, in_=src[:, :]), 16)
            for c in range(NCH):
                sX0.inc(sync.dma_start(out=xt[:, c * D:(c + 1) * D],
                                       in_=x_h[c * S:(c + 1) * S, :]), 16)

        LD0 = sLD.n

        @blk.gpsimd
        def _(g):
            for dch in range(DCH):
                sGL.inc(g.dma_start(out=btre[:, dch * H:(dch + 1) * H],
                                    in_=btre_h[dch * 128:(dch + 1) * 128, :]), 16)
                sGL.inc(g.dma_start(out=btim[:, dch * H:(dch + 1) * H],
                                    in_=btim_h[dch * 128:(dch + 1) * 128, :]), 16)
            for hc in range(HCH):
                sGL.inc(g.dma_start(out=ctre[:, hc * D:(hc + 1) * D],
                                    in_=ctre_h[hc * 128:(hc + 1) * 128, :]), 16)
                sGL.inc(g.dma_start(out=ctimn[:, hc * D:(hc + 1) * D],
                                    in_=ctim_h[hc * 128:(hc + 1) * 128, :]), 16)

        GL0 = sGL.n

        @blk.scalar
        def _(s):
            s.wait_ge(sLD.h, LD0)
            sPS.inc(s.activation(nu_c[:, :], nucol[:, :], AF.Exp))
            sPS.inc(s.activation(sm8a[:, :], thcol[:, :], AF.Exp))
            sPS.inc(s.activation(nrow2, nurow, AF.Exp))
            sPS.inc(s.activation(rD[0:1, :], throw, AF.Exp))
        PS1 = sPS.n

        @blk.vector
        def _(v):
            v.wait_ge(sPS.h, PS1)
            v.memset(gsc[:, :], 0.0)
            sPV.inc(v.tensor_scalar(th2p_c[:, :], sm8a[:, :], INV_2PI, None, ALU.mult))
            sPV.inc(v.tensor_scalar(nneg_c[:, :], nu_c[:, :], -1.0, None, ALU.mult))
            sPV.inc(v.tensor_scalar(t2prow, rD[0:1, :], INV_2PI, None, ALU.mult))
        PV1 = sPV.n

        @blk.scalar
        def _(s):
            s.wait_ge(sPV.h, PV1)
            sPS.inc(s.activation(grow, nrow2, AF.Exp, scale=-2.0))
        PS2 = sPS.n

        @blk.vector
        def _(v):
            v.wait_ge(sPS.h, PS2)
            sPV.inc(v.tensor_scalar(growtmp, grow, -1.0, 1.0 + 1e-5, ALU.mult, ALU.add))
        PV2 = sPV.n

        @blk.scalar
        def _(s):
            s.wait_ge(sPV.h, PV2)
            sPS.inc(s.activation(grow, growtmp, AF.Sqrt))
        PS3 = sPS.n

        @blk.vector
        def _(v):
            v.wait_ge(sPS.h, PS3)
            sPV.inc(v.tensor_scalar(d1row, dskrow, 1.0, None, ALU.add))
        PV2b = sPV.n

        @blk.tensor
        def _(t):
            t.wait_ge(sPV.h, PV2b)
            for ps, row in [(pA, grow), (pB, nrow2), (pHr, t2prow), (pHi, d1row)]:
                t.matmul(ps[:, 0:512], onesr[:, :], row[0:1, 0:512], start=True, stop=True)
                sPT.inc(t.matmul(ps[:, 512:1024], onesr[:, :], row[0:1, 512:1024],
                                 start=True, stop=True))
        PT1 = sPT.n

        @blk.vector
        def _(v):
            v.wait_ge(sPT.h, PT1)
            sPV.inc(v.tensor_copy(scr4[:, :], pA[:, :]))
            sPV.inc(v.tensor_copy(nbc[:, :], pB[:, :]))
            sPV.inc(v.tensor_copy(t2pbc[:, :], pHr[:, :]))
            sPV.inc(v.tensor_copy(d1bc[:, :], pHi[:, :]))
            v.wait_ge(sGL.h, GL0)
            for dch in range(DCH):
                sPV.inc(v.tensor_tensor(btre[:, dch * H:(dch + 1) * H],
                                        btre[:, dch * H:(dch + 1) * H], scr4[:, :], ALU.mult))
                sPV.inc(v.tensor_tensor(btim[:, dch * H:(dch + 1) * H],
                                        btim[:, dch * H:(dch + 1) * H], scr4[:, :], ALU.mult))
            sPV.inc(v.tensor_scalar(ctimn[:, :], ctimn[:, :], -1.0, None, ALU.mult))
        PV3 = sPV.n

        @blk.scalar
        def _(s):
            s.wait_ge(sPV.h, PV3)
            sPS.inc(s.activation(scr2[:, :], t2pbc[:, :], AF.Identity, scale=iotac[:, :]))
            sPS.inc(s.activation(scr3[:, :], nbc[:, :], AF.Exp, scale=iotac[:, :]))
        PS4 = sPS.n

        @blk.vector
        def _(v):
            v.wait_ge(sPS.h, PS4)
            sPV.inc(v.tensor_scalar(scri[:, :], scr2[:, :], 1.0, None, ALU.mult))
            sPV.inc(v.tensor_tensor(scr2[:, :], scr2[:, :], scri[:, :], ALU.subtract))
            sPV.inc(v.tensor_scalar(scri[:, :], scr2[:, :], 0.5, None, ALU.is_ge))
            sPV.inc(v.tensor_tensor(scr2[:, :], scr2[:, :], scri[:, :], ALU.subtract))
            sPV.inc(v.tensor_scalar(scr4[:, :], scr2[:, :], -1.0, 0.25, ALU.mult, ALU.add))
            sPV.inc(v.tensor_scalar(scri[:, :], scr4[:, :], 0.5, None, ALU.is_ge))
            sPV.inc(v.tensor_tensor(scr4[:, :], scr4[:, :], scri[:, :], ALU.subtract))
        PV4 = sPV.n

        @blk.scalar
        def _(s):
            s.wait_ge(sPV.h, PV4)
            sPS.inc(s.activation(nbc[:, :], scr2[:, :], AF.Sin, scale=TWO_PI))
            sPS.inc(s.activation(t2pbc[:, :], scr4[:, :], AF.Sin, scale=TWO_PI))
        PS5 = sPS.n

        @blk.vector
        def _(v):
            v.wait_ge(sPS.h, PS5)
            sPV.inc(v.tensor_tensor(tnre[:, :], scr3[:, :], t2pbc[:, :], ALU.mult))
            sPV.inc(v.scalar_tensor_tensor(tnim[:, :], scr3[:, :], -1.0, nbc[:, :],
                                           ALU.mult, ALU.mult))
            sPV.inc(v.tensor_tensor(r3(scr2), r3(iotacat), bc3(th2p_c, 128), ALU.mult))
            sPV.inc(v.tensor_tensor(r3(scr3), r3(iotacat), bc3(nneg_c, 128), ALU.mult))
        PV5 = sPV.n

        @blk.scalar
        def _(s):
            s.wait_ge(sPV.h, PV5)
            sPS.inc(s.activation(scr3[:, :], scr3[:, :], AF.Exp))
        PS6 = sPS.n

        @blk.vector
        def _(v):
            v.wait_ge(sPS.h, PS6)
            sPV.inc(v.tensor_scalar(scri[:, :], scr2[:, :], 1.0, None, ALU.mult))
            sPV.inc(v.tensor_tensor(scr2[:, :], scr2[:, :], scri[:, :], ALU.subtract))
            sPV.inc(v.tensor_scalar(scri[:, :], scr2[:, :], 0.5, None, ALU.is_ge))
            sPV.inc(v.tensor_tensor(scr2[:, :], scr2[:, :], scri[:, :], ALU.subtract))
            sPV.inc(v.tensor_scalar(scr4[:, :], scr2[:, :], -1.0, 0.25, ALU.mult, ALU.add))
            sPV.inc(v.tensor_scalar(scri[:, :], scr4[:, :], 0.5, None, ALU.is_ge))
            sPV.inc(v.tensor_tensor(scr4[:, :], scr4[:, :], scri[:, :], ALU.subtract))
            fv = mkfence(v, sFV)
            fv(v.tensor_scalar(sm8b[:, :], th2p_c[:, :], 128.0, None, ALU.mult))
            fv(v.tensor_scalar(sm8i[:, :], sm8b[:, :], 1.0, None, ALU.mult))
            fv(v.tensor_tensor(sm8b[:, :], sm8b[:, :], sm8i[:, :], ALU.subtract))
            fv(v.tensor_scalar(sm8i[:, :], sm8b[:, :], 0.5, None, ALU.is_ge))
            fv(v.tensor_tensor(sm8b[:, :], sm8b[:, :], sm8i[:, :], ALU.subtract))
            fv(v.tensor_scalar(sm8c[:, :], sm8b[:, :], -1.0, 0.25, ALU.mult, ALU.add))
            fv(v.tensor_scalar(sm8i[:, :], sm8c[:, :], 0.5, None, ALU.is_ge))
            sPV.inc(v.tensor_tensor(sm8c[:, :], sm8c[:, :], sm8i[:, :], ALU.subtract))
        PV6 = sPV.n

        @blk.scalar
        def _(s):
            s.wait_ge(sPV.h, PV6)
            sPS.inc(s.activation(nbc[:, :], scr2[:, :], AF.Sin, scale=TWO_PI))
            sPS.inc(s.activation(t2pbc[:, :], scr4[:, :], AF.Sin, scale=TWO_PI))
            sPS.inc(s.activation(sm8b[:, :], sm8b[:, :], AF.Sin, scale=TWO_PI))
            sPS.inc(s.activation(sm8c[:, :], sm8c[:, :], AF.Sin, scale=TWO_PI))
            sPS.inc(s.activation(sm8a[:, :], nu_c[:, :], AF.Exp, scale=-128.0))
        PS7 = sPS.n

        @blk.vector
        def _(v):
            v.wait_ge(sPS.h, PS7)
            sPV.inc(v.tensor_tensor(tpre[:, :], scr3[:, :], t2pbc[:, :], ALU.mult))
            sPV.inc(v.tensor_tensor(tpim[:, :], scr3[:, :], nbc[:, :], ALU.mult))
            sPV.inc(v.tensor_tensor(tp127re[:, :],
                                    r3(scr3)[:, :, 127:128].squeeze(2),
                                    r3(t2pbc)[:, :, 127:128].squeeze(2), ALU.mult))
            sPV.inc(v.tensor_tensor(tp127im[:, :],
                                    r3(scr3)[:, :, 127:128].squeeze(2),
                                    r3(nbc)[:, :, 127:128].squeeze(2), ALU.mult))
            sPV.inc(v.tensor_tensor(phre[:, :], sm8a[:, :], sm8c[:, :], ALU.mult))
            sPV.inc(v.tensor_tensor(phim[:, :], sm8a[:, :], sm8b[:, :], ALU.mult))
            sPV.inc(v.memset(lcre[0][:, :], 0.0))
            sPV.inc(v.memset(lcim[0][:, :], 0.0))
        PV7 = sPV.n

        wp_tiles = []
        for ti in range(3):
            wp_tiles.append(("up", ti))
        for ti in range(3):
            wp_tiles.append(("gt", ti))
        for ti in range(8):
            wp_tiles.append(("dn", ti))

        @blk.gpsimd
        def _(g):
            g.wait_ge(sGL.h, GL0)
            for wi, (w, ti) in enumerate(wp_tiles):
                if w == "up":
                    src = wup_h[ti * 128:(ti + 1) * 128, :]
                    dst = wld[wi % 2][:, 0:1024]
                elif w == "gt":
                    src = wgt_h[ti * 128:(ti + 1) * 128, :]
                    dst = wld[wi % 2][:, 0:1024]
                else:
                    src = wdn_h[ti * 128:(ti + 1) * 128, :]
                    dst = wld[wi % 2][:, 0:ISL]
                if wi >= 1:
                    g.wait_ge(sPV.h, PV7 + 2 * wi - 1)
                else:
                    g.wait_ge(sPV.h, PV3)
                sWL.inc(g.dma_start(out=dst, in_=src), 16)

        @blk.vector
        def _(v):
            v.wait_ge(sPV.h, PV7)
            for wi, (w, ti) in enumerate(wp_tiles):
                n = 1024 if w != "dn" else ISL
                nb = n // 32
                buf = wld[wi % 2][:, 0:n]
                v.wait_ge(sWL.h, 16 * (wi + 1))
                bm = bmaxb[:, 0:nb]
                bs = bsclb[:, 0:nb]
                bi = binvb[:, 0:nb]
                fv = mkfence(v, sFV)
                fv(v.tensor_reduce(bm, r3(buf, 32), AX.X, ALU.max,
                                   apply_absolute_value=True))
                fv(v.tensor_scalar(bs, bm, 1.0 / 7.0, 1e-12, ALU.mult, ALU.max))
                fv(v.reciprocal(bi, bs))
                v.tensor_tensor(r3(scri[:, 0:n], 32), r3(buf, 32), bc3(bi, 32), ALU.mult)
                sPV.inc(v.tensor_tensor(r3(wqbf[:, 0:n], 32), r3(scri[:, 0:n], 32),
                                        bc3(bs, 32), ALU.mult))
                v.wait_ge(sPT.h, PT1 + wi + 1)
                sPV.inc(v.tensor_copy(wtrs[:, 0:n], pA[:, :].bitcast(BF16)[:, 0:n]))

        @blk.tensor
        def _(t):
            for wi, (w, ti) in enumerate(wp_tiles):
                n = 1024 if w != "dn" else ISL
                t.wait_ge(sPV.h, PV7 + 2 * wi + 1)
                if wi >= 1:
                    t.wait_ge(sPV.h, PV7 + 2 * wi)
                pab = pA[:, :].bitcast(BF16)
                for j in range(n // 128):
                    ins = t.transpose(pab[:, j * 128:(j + 1) * 128],
                                      wqbf[:, j * 128:(j + 1) * 128], identb[:, :])
                sPT.inc(ins)
        PT_W = sPT.n

        @blk.sync
        def _(sync):
            for wi, (w, ti) in enumerate(wp_tiles):
                n = 1024 if w != "dn" else ISL
                sync.wait_ge(sPV.h, PV7 + 2 * wi + 2)
                if w == "up":
                    dst = agu_in[:, ti * 128:(ti + 1) * 128].rearrange(
                        "(dc p) i -> p dc i", p=128)
                elif w == "gt":
                    dst = agg_in[:, ti * 128:(ti + 1) * 128].rearrange(
                        "(dc p) i -> p dc i", p=128)
                else:
                    dst = agd_in[:, ti * 128:(ti + 1) * 128].rearrange(
                        "(isub p) d2 -> p isub d2", p=128)
                sPD.inc(sync.dma_start(out=dst, in_=r3(wtrs[:, 0:n], 128)), 16)
        PD_W = sPD.n
        PV_W = sPV.n

        @blk.gpsimd
        def _(g):
            g.wait_ge(sPD.h, PD_W)
            sAG.inc(g.collective_compute("AllGather", ALU.bypass, replica_groups=RG,
                                         ins=[agu_in[:, :]], outs=[agu_out[:, :]]))
            sAG.inc(g.collective_compute("AllGather", ALU.bypass, replica_groups=RG,
                                         ins=[agg_in[:, :]], outs=[agg_out[:, :]]))
            sAG.inc(g.collective_compute("AllGather", ALU.bypass, replica_groups=RG,
                                         ins=[agd_in[:, :]], outs=[agd_out[:, :]]))
        AG_W = sAG.n

        _scope_end()
        _scope_begin("attn")
        @blk.tensor
        def _(t):
            t.wait_ge(sPV.h, PV_W)
            t.wait_ge(sX0.h, 16 * NCH)
            for dch in range(DCH):
                ins = t.transpose(pA[:, dch * 128:(dch + 1) * 128],
                                  xt[:, dch * 128:(dch + 1) * 128], identf[:, :])
            sXT.inc(ins)
            t.wait_ge(sXF.h, 1)
            for half in range(2):
                for dch in range(DCH):
                    ins = t.matmul(pA[:, half * 512:(half + 1) * 512],
                                   x0f[:, dch * 128:(dch + 1) * 128],
                                   btre[:, dch * H + half * 512:dch * H + half * 512 + 512],
                                   start=(dch == 0), stop=(dch == DCH - 1))
            for half in range(2):
                for dch in range(DCH):
                    ins = t.matmul(pB[:, half * 512:(half + 1) * 512],
                                   x0f[:, dch * 128:(dch + 1) * 128],
                                   btim[:, dch * H + half * 512:dch * H + half * 512 + 512],
                                   start=(dch == 0), stop=(dch == DCH - 1))
            sBU.inc(ins)
            for c in range(NCH):
                if c >= 1:
                    t.wait_ge(sHS.h, c)
                    t.wait_ge(sY.h, c - 1)
                    for half in range(2):
                        for hc in range(HCH):
                            t.matmul(pC[:, half * 512:(half + 1) * 512],
                                     hsre[:, hc * 128:(hc + 1) * 128],
                                     ctre[:, hc * D + half * 512:hc * D + half * 512 + 512],
                                     start=(hc == 0), stop=False)
                            ins = t.matmul(
                                pC[:, half * 512:(half + 1) * 512],
                                hsim[:, hc * 128:(hc + 1) * 128],
                                ctimn[:, hc * D + half * 512:hc * D + half * 512 + 512],
                                start=False, stop=(hc == HCH - 1))
                    sCP.inc(ins)
                t.wait_ge(sZ.h, c + 1)
                t.wait_ge(sHSu.h, 2 * c)
                for j in range(4):
                    t.matmul(pHx[:, j * 128:(j + 1) * 128],
                             zre[:, j * 128:(j + 1) * 128], ltones[:, :],
                             start=True, stop=True)
                    ins = t.matmul(pHx[:, 512 + j * 128:512 + (j + 1) * 128],
                                   zim[:, j * 128:(j + 1) * 128], ltones[:, :],
                                   start=True, stop=True)
                sHm.inc(ins)
                if c + 1 < NCH:
                    t.wait_ge(sZp.h, 2 * c + 2)
                    for dch in range(DCH):
                        ins = t.transpose(pA[:, dch * 128:(dch + 1) * 128],
                                          xt[:, (c + 1) * D + dch * 128:
                                             (c + 1) * D + (dch + 1) * 128],
                                          identf[:, :])
                    sXT.inc(ins)
                    t.wait_ge(sXF.h, c + 2)
                    for half in range(2):
                        for dch in range(DCH):
                            ins = t.matmul(
                                pA[:, half * 512:(half + 1) * 512],
                                x0f[:, dch * 128:(dch + 1) * 128],
                                btre[:, dch * H + half * 512:dch * H + half * 512 + 512],
                                start=(dch == 0), stop=(dch == DCH - 1))
                    for half in range(2):
                        for dch in range(DCH):
                            ins = t.matmul(
                                pB[:, half * 512:(half + 1) * 512],
                                x0f[:, dch * 128:(dch + 1) * 128],
                                btim[:, dch * H + half * 512:dch * H + half * 512 + 512],
                                start=(dch == 0), stop=(dch == DCH - 1))
                    sBU.inc(ins)
                t.wait_ge(sHSu.h, 2 * c + 1)
                for j in range(4):
                    t.matmul(pHx[:, j * 128:(j + 1) * 128],
                             zre[:, 512 + j * 128:512 + (j + 1) * 128], ltones[:, :],
                             start=True, stop=True)
                    ins = t.matmul(pHx[:, 512 + j * 128:512 + (j + 1) * 128],
                                   zim[:, 512 + j * 128:512 + (j + 1) * 128], ltones[:, :],
                                   start=True, stop=True)
                sHm.inc(ins)
            t.wait_ge(sHS.h, NCH)
            t.wait_ge(sY.h, NCH - 1)
            for half in range(2):
                for hc in range(HCH):
                    t.matmul(pC[:, half * 512:(half + 1) * 512],
                             hsre[:, hc * 128:(hc + 1) * 128],
                             ctre[:, hc * D + half * 512:hc * D + half * 512 + 512],
                             start=(hc == 0), stop=False)
                    ins = t.matmul(pC[:, half * 512:(half + 1) * 512],
                                   hsim[:, hc * 128:(hc + 1) * 128],
                                   ctimn[:, hc * D + half * 512:hc * D + half * 512 + 512],
                                   start=False, stop=(hc == HCH - 1))
            sCP.inc(ins)

        @blk.scalar
        def _(s):
            for c in range(NCH):
                s.wait_ge(sXT.h, c + 1)
                sXF.inc(s.activation(x0f[:, :], pA[:, :], AF.Identity))
                if c >= 1:
                    s.wait_ge(sY.h, c)
                    sSQ.inc(s.activation(scr4[:, :], scr3[:, :], AF.Square,
                                         accum_out=sumsq[:, :]))
                    s.wait_ge(sVR.h, c)
                    sSD.inc(s.activation(sdb[:, :], varb[:, :], AF.Sqrt))
            s.wait_ge(sY.h, NCH)
            sSQ.inc(s.activation(scr4[:, :], scr3[:, :], AF.Square, accum_out=sumsq[:, :]))
            s.wait_ge(sVR.h, NCH)
            sSD.inc(s.activation(sdb[:, :], varb[:, :], AF.Sqrt))

        @blk.gpsimd
        def _(g):
            for c in range(NCH):
                g.wait_ge(sZp.h, 2 * c + 2)
                g.wait_ge(sHm.h, 2 * c)
                g.tensor_tensor(zre[:, :], scr5g[:, :], scr6g[:, :], ALU.subtract)
                sZ.inc(g.tensor_tensor(zim[:, :], scr7g[:, :], scr8g[:, :], ALU.add))
                g.wait_ge(sX0.h, 16 * NCH)
                g.wait_ge(sLN.h, c)
                g.wait_ge(sPV.h, PV3)
                sT1.inc(g.tensor_tensor(t1g[:, :], xt[:, c * D:(c + 1) * D], d1bc[:, :],
                                        ALU.mult))

        @blk.vector
        def _(v):
            v.wait_ge(sBU.h, 1)
            v.tensor_tensor(scr5g[:, :], tnre[:, :], pA[:, :], ALU.mult)
            sZp.inc(v.tensor_tensor(scr6g[:, :], tnim[:, :], pB[:, :], ALU.mult))
            v.tensor_tensor(scr7g[:, :], tnre[:, :], pB[:, :], ALU.mult)
            sZp.inc(v.tensor_tensor(scr8g[:, :], tnim[:, :], pA[:, :], ALU.mult))
            for c in range(NCH):
                par = c % 2
                for r in range(2):
                    o5 = 512 * r
                    v.wait_ge(sHm.h, 2 * c + r + 1)
                    v.tensor_tensor(r3(scr1[:, o5:o5 + 512]), r3(pHx[:, 0:512]),
                                    bc3(lcre[par][:, 4 * r:4 * r + 4], 128), ALU.add)
                    sHSu.inc(v.tensor_tensor(r3(scr2[:, o5:o5 + 512]), r3(pHx[:, 512:1024]),
                                             bc3(lcim[par][:, 4 * r:4 * r + 4], 128), ALU.add))
                v.tensor_tensor(scr3[:, :], tpre[:, :], scr1[:, :], ALU.mult)
                v.tensor_tensor(scr4[:, :], tpim[:, :], scr2[:, :], ALU.mult)
                v.tensor_tensor(hsre[:, :], scr3[:, :], scr4[:, :], ALU.subtract)
                v.tensor_tensor(scr3[:, :], tpim[:, :], scr1[:, :], ALU.mult)
                v.tensor_tensor(scr4[:, :], tpre[:, :], scr2[:, :], ALU.mult)
                v.tensor_tensor(hsim[:, :], scr3[:, :], scr4[:, :], ALU.add)
                u127re = r3(scr1)[:, :, 127:128].squeeze(2)
                u127im = r3(scr2)[:, :, 127:128].squeeze(2)
                fv = mkfence(v, sFV)
                v.tensor_tensor(sm8a[:, :], phre[:, :], u127re, ALU.mult)
                fv(v.tensor_tensor(sm8b[:, :], phim[:, :], u127im, ALU.mult))
                fv(v.tensor_tensor(lcre[1 - par][:, :], sm8a[:, :], sm8b[:, :], ALU.subtract))
                v.tensor_tensor(sm8a[:, :], phim[:, :], u127re, ALU.mult)
                fv(v.tensor_tensor(sm8b[:, :], phre[:, :], u127im, ALU.mult))
                sHS.inc(v.tensor_tensor(lcim[1 - par][:, :], sm8a[:, :], sm8b[:, :], ALU.add))
                if c + 1 < NCH:
                    v.wait_ge(sBU.h, c + 2)
                    v.wait_ge(sZ.h, c + 1)
                    v.tensor_tensor(scr5g[:, :], tnre[:, :], pA[:, :], ALU.mult)
                    sZp.inc(v.tensor_tensor(scr6g[:, :], tnim[:, :], pB[:, :], ALU.mult))
                    v.tensor_tensor(scr7g[:, :], tnre[:, :], pB[:, :], ALU.mult)
                    sZp.inc(v.tensor_tensor(scr8g[:, :], tnim[:, :], pA[:, :], ALU.mult))
                v.wait_ge(sCP.h, c + 1)
                v.wait_ge(sT1.h, c + 1)
                sY.inc(v.scalar_tensor_tensor(scr3[:, :], t1g[:, :], 1.0, pC[:, :],
                                              ALU.mult, ALU.add, accum_out=sums[:, :]))
                v.wait_ge(sSQ.h, c + 1)
                fv = mkfence(v, sFV)
                v.tensor_scalar(mean[:, :], sums[:, :], 1.0 / D, None, ALU.mult)
                fv(v.tensor_scalar(ex2[:, :], sumsq[:, :], 1.0 / D, None, ALU.mult))
                fv(v.tensor_tensor(t_c1[:, :], mean[:, :], mean[:, :], ALU.mult))
                fv(v.tensor_tensor(varb[:, :], ex2[:, :], t_c1[:, :], ALU.subtract))
                sVR.inc(v.tensor_scalar(varb[:, :], varb[:, :], 1e-5, None, ALU.add))
                v.wait_ge(sSD.h, c + 1)
                fv(v.reciprocal(rstd[:, :], sdb[:, :]))
                v.tensor_scalar(xt[:, c * D:(c + 1) * D], scr3[:, :], mean[:, :], rstd[:, :],
                                ALU.subtract, ALU.mult)
                fv(v.tensor_reduce(tred[:, :], xt[:, c * D:(c + 1) * D], AX.X, ALU.max,
                                   apply_absolute_value=True))
                if c == 0:
                    sLN.inc(v.tensor_copy(rmax_x1[:, :], tred[:, :]))
                else:
                    sLN.inc(v.tensor_tensor(rmax_x1[:, :], rmax_x1[:, :], tred[:, :], ALU.max))

        _scope_end()
        _scope_begin("ar1")
        @blk.tensor
        def _(t):
            t.wait_ge(sLN.h, NCH)
            sPT.inc(t.transpose(pHr[0:1, 0:128], rmax_x1[:, :], identf[:, :]))
        PT_A1 = sPT.n

        @blk.vector
        def _(v):
            v.wait_ge(sPT.h, PT_A1)
            fv = mkfence(v, sFV)
            fv(v.tensor_copy(rowt[:, :], pHr[0:1, 0:128]))
            sLN.inc(v.tensor_reduce(gsc[0:1, 0:1], rowt[0:1, :], AX.X, ALU.max))
        LN_A1 = sLN.n

        @blk.gpsimd
        def _(g):
            g.wait_ge(sAG.h, AG_W)
            g.wait_ge(sLN.h, LN_A1)
            sGL.inc(g.dma_start(out=ccin[0][0:1, 0:16], in_=gsc[0:1, 0:16]), 16)
            g.wait_ge(sGL.h, sGL.n)
            sAR.inc(g.collective_compute("AllReduce", ALU.max, replica_groups=RG,
                                         ins=[ccin[0][:, :]], outs=[ccout[0][:, :]]))
            g.wait_ge(sAR.h, sAR.n)
            sGL.inc(g.dma_start(out=gsc[0:1, 0:16], in_=ccout[0][0:1, 0:16]), 16)
        GL_A1 = sGL.n

        @blk.tensor
        def _(t):
            t.wait_ge(sGL.h, GL_A1)
            sPT.inc(t.matmul(pHi[:, 0:16], onesr[:, :], gsc[0:1, 0:16], start=True, stop=True))
        PT_B1 = sPT.n

        @blk.vector
        def _(v):
            v.wait_ge(sPT.h, PT_B1)
            fv = mkfence(v, sFV)
            fv(v.tensor_copy(bcast[:, :], pHi[:, 0:16]))
            fv(v.tensor_scalar(s_x[:, :], bcast[:, 0:1], 1.0 / QMAX, 1e-12, ALU.mult, ALU.max))
            sBC.inc(v.reciprocal(r_x[:, :], s_x[:, :]))
        BC1 = sBC.n

        es_a0.close()
        es_pa.close()

        if stage == "x1":
            @blk.sync
            def _(sync):
                sync.wait_ge(sBC.h, BC1)
                for c in range(NCH):
                    sOUT.inc(sync.dma_start(out=out_h[c * S:(c + 1) * S, :],
                                            in_=xt[:, c * D:(c + 1) * D]), 16)
                sync.wait_ge(sOUT.h, sOUT.n)
            es_a.close()
            es_ab.close()
            return nc

        es_a.close()

        es_b = ExitStack()
        eb = es_b.enter_context
        xqf = eb(nc.sbuf_tensor("xqf", [128, DCH * T], BF16))
        qint = eb(nc.sbuf_tensor("qint", [128, 1024], I32))
        xqt = eb(nc.sbuf_tensor("xqt", [128, 1024], BF16))
        stup = [eb(nc.sbuf_tensor(f"stup{i}", [128, 1024], BF16)) for i in range(2)]
        stgt = [eb(nc.sbuf_tensor(f"stgt{i}", [128, 1024], BF16)) for i in range(2)]
        stagU = [eb(nc.sbuf_tensor(f"stagU{i}", [128, 512], F32)) for i in range(2)]
        stagG = [eb(nc.sbuf_tensor(f"stagG{i}", [128, 512], F32)) for i in range(2)]

        es_pb = ExitStack()
        pT1 = es_pb.enter_context(nc.psum_tensor("pT1", [128, 1024], F32))
        pU = [es_pb.enter_context(nc.psum_tensor(f"pU{i}", [128, 512], F32)) for i in range(3)]
        pG = [es_pb.enter_context(nc.psum_tensor(f"pG{i}", [128, 512], F32)) for i in range(3)]

        _scope_end()
        _scope_begin("ffn0")
        @blk.vector
        def _(v):
            v.wait_ge(sBC.h, BC1)
            for c in range(NCH):
                v.tensor_scalar(qint[:, :], xt[:, c * D:(c + 1) * D], r_x[:, :], None, ALU.mult)
                sQ1.inc(v.tensor_scalar(xqt[:, :], qint[:, :], s_x[:, :], None, ALU.mult))
                v.wait_ge(sQ2.h, c + 1)
                sQ3.inc(v.tensor_copy(
                    xqf[:, :].rearrange("p (dc t) -> p dc t", t=T)[:, :, c * 128:(c + 1) * 128],
                    r3(pT1[:, :].bitcast(BF16)[:, 0:1024])))

        @blk.tensor
        def _(t):
            for c in range(NCH):
                t.wait_ge(sQ1.h, c + 1)
                if c >= 1:
                    t.wait_ge(sQ3.h, c)
                pqb = pT1[:, :].bitcast(BF16)
                for dch in range(DCH):
                    ins = t.transpose(pqb[:, dch * 128:(dch + 1) * 128],
                                      xqt[:, dch * 128:(dch + 1) * 128], identb[:, :])
                sQ2.inc(ins)

        @blk.sync
        def _(sync):
            sync.wait_ge(sQ3.h, NCH)
            for c in range(NCH):
                sSPX.inc(sync.dma_start(out=x1dram[:, c * D:(c + 1) * D],
                                        in_=xt[:, c * D:(c + 1) * D]), 16)
        SP_X1 = sSPX.n

        _scope_end()
        _scope_begin("ffn1")
        @blk.gpsimd
        def _(g):
            for ic in range(ICH):
                blkid, col = ic // 3, (ic % 3) * 128
                if ic >= 2:
                    g.wait_ge(sMM.h, (ic - 1) * NTS * 2)
                src_u = agu_out[blkid * D:(blkid + 1) * D, col:col + 128].rearrange(
                    "(dc p) i -> p dc i", p=128)
                src_g = agg_out[blkid * D:(blkid + 1) * D, col:col + 128].rearrange(
                    "(dc p) i -> p dc i", p=128)
                sem = sWS if ic % 2 == 0 else sWS1
                sem.inc(g.dma_start(out=r3(stup[ic % 2], 128), in_=src_u), 16)
                sem.inc(g.dma_start(out=r3(stgt[ic % 2], 128), in_=src_g), 16)

        @blk.tensor
        def _(t):
            t.wait_ge(sQ3.h, NCH)
            for ic in range(ICH):
                sem = sWS if ic % 2 == 0 else sWS1
                t.wait_ge(sem.h, 32 * (ic // 2 + 1))
                for ts_ in range(4):
                    mmi = ic * NTS + ts_
                    par = mmi % 3
                    if mmi >= 3:
                        t.wait_ge(sEU.h, mmi - 2)
                        t.wait_ge(sEG.h, mmi - 2)
                    for dch in range(DCH):
                        t.matmul(pU[par][:, :], stup[ic % 2][:, dch * 128:(dch + 1) * 128],
                                 xqf[:, dch * T + ts_ * 512:dch * T + ts_ * 512 + 512],
                                 start=(dch == 0), stop=(dch == DCH - 1))
                    for dch in range(DCH):
                        ins = t.matmul(pG[par][:, :], stgt[ic % 2][:, dch * 128:(dch + 1) * 128],
                                       xqf[:, dch * T + ts_ * 512:dch * T + ts_ * 512 + 512],
                                       start=(dch == 0), stop=(dch == DCH - 1))
                    sMM.inc(ins, 2)

        @blk.scalar
        def _(s):
            for mmi in range(ICH * NTS):
                s.wait_ge(sMM.h, 2 * (mmi + 1))
                if mmi >= 2:
                    s.wait_ge((sSP if mmi % 2 == 0 else sSP1).h, 32 * (mmi // 2))
                sEU.inc(s.activation(stagU[mmi % 2][:, :], pU[mmi % 3][:, :], AF.Identity))

        @blk.vector
        def _(v):
            for mmi in range(ICH * NTS):
                v.wait_ge(sMM.h, 2 * (mmi + 1))
                if mmi >= 2:
                    v.wait_ge((sSP if mmi % 2 == 0 else sSP1).h, 32 * (mmi // 2))
                sEG.inc(v.tensor_copy(stagG[mmi % 2][:, :], pG[mmi % 3][:, :]))
                v.tensor_reduce(tredB[:, mmi:mmi + 1], stagG[mmi % 2][:, :], AX.X, ALU.max,
                                apply_absolute_value=True)
                v.wait_ge(sEU.h, mmi + 1)
                sRM.inc(v.tensor_reduce(tredA[:, mmi:mmi + 1], stagU[mmi % 2][:, :], AX.X,
                                        ALU.max, apply_absolute_value=True))

        @blk.sync
        def _(sync):
            for mmi in range(ICH * NTS):
                ic, ts_ = mmi // NTS, mmi % NTS
                par = mmi % 2
                sem = sSP if par == 0 else sSP1
                sync.wait_ge(sEU.h, mmi + 1)
                sem.inc(sync.dma_start(
                    out=updram[ic * 128:(ic + 1) * 128, ts_ * 512:ts_ * 512 + 512],
                    in_=stagU[par][:, :]), 16)
                sync.wait_ge(sRM.h, mmi + 1)
                sem.inc(sync.dma_start(
                    out=gatedram[ic * 128:(ic + 1) * 128, ts_ * 512:ts_ * 512 + 512],
                    in_=stagG[par][:, :]), 16)
        SP_ALL = (sSP.n, sSP1.n, sSPX.n)

        _scope_end()
        _scope_begin("ar2")
        @blk.vector
        def _(v):
            v.wait_ge(sRM.h, ICH * NTS)
            fv = mkfence(v, sFV)
            fv(v.tensor_reduce(rmax_up[:, :], tredA[:, :], AX.X, ALU.max))
            sRM.inc(v.tensor_reduce(rmax_gt[:, :], tredB[:, :], AX.X, ALU.max))

        @blk.tensor
        def _(t):
            t.wait_ge(sRM.h, ICH * NTS + 1)
            t.transpose(pT1[0:1, 0:128], rmax_up[:, :], identf[:, :])
            sPT.inc(t.transpose(pT1[0:1, 128:256], rmax_gt[:, :], identf[:, :]))
        PT_A2 = sPT.n

        @blk.vector
        def _(v):
            v.wait_ge(sPT.h, PT_A2)
            fv = mkfence(v, sFV)
            fv(v.tensor_copy(rowt[:, :], pT1[0:1, 0:128]))
            fv(v.tensor_reduce(gsc[0:1, 0:1], rowt[0:1, :], AX.X, ALU.max))
            fv(v.tensor_copy(rowt[:, :], pT1[0:1, 128:256]))
            sBC.inc(v.tensor_reduce(gsc[0:1, 1:2], rowt[0:1, :], AX.X, ALU.max))
        BC_A2 = sBC.n

        @blk.gpsimd
        def _(g):
            g.wait_ge(sBC.h, BC_A2)
            sGL.inc(g.dma_start(out=ccin[1][0:1, 0:16], in_=gsc[0:1, 0:16]), 16)
            g.wait_ge(sGL.h, sGL.n)
            sAR.inc(g.collective_compute("AllReduce", ALU.max, replica_groups=RG,
                                         ins=[ccin[1][:, :]], outs=[ccout[1][:, :]]))
            g.wait_ge(sAR.h, sAR.n)
            sGL.inc(g.dma_start(out=gsc[0:1, 0:16], in_=ccout[1][0:1, 0:16]), 16)
        GL_A2 = sGL.n

        @blk.tensor
        def _(t):
            t.wait_ge(sGL.h, GL_A2)
            sPT.inc(t.matmul(pT1[:, 0:16], onesr[:, :], gsc[0:1, 0:16], start=True, stop=True))
        PT_B2 = sPT.n

        @blk.vector
        def _(v):
            v.wait_ge(sPT.h, PT_B2)
            fv = mkfence(v, sFV)
            fv(v.tensor_copy(bcast[:, :], pT1[:, 0:16]))
            fv(v.tensor_scalar(s_up[:, :], bcast[:, 0:1], 1.0 / QMAX, 1e-12, ALU.mult, ALU.max))
            v.reciprocal(r_up[:, :], s_up[:, :])
            fv(v.tensor_scalar(s_gt[:, :], bcast[:, 1:2], 1.0 / QMAX, 1e-12, ALU.mult, ALU.max))
            fv(v.reciprocal(r_gt[:, :], s_gt[:, :]))
            sBC.inc(v.tensor_scalar(sg216[:, :], s_gt[:, :], 1.0 / 65536.0, None, ALU.mult))
        BC2 = sBC.n


        def early_out(gates):
            @blk.sync
            def _(sync):
                for sem, val in gates:
                    sync.wait_ge(sem.h, val)
                for c in range(NCH):
                    sOUT.inc(sync.dma_start(out=out_h[c * S:(c + 1) * S, :],
                                            in_=x1dram[:, c * D:(c + 1) * D]), 16)
                sync.wait_ge(sOUT.h, sOUT.n)

        if stage == "f2":
            early_out([(sBC, BC2), (sSP, SP_ALL)])
            es_pb.close()
            es_b.close()
            es_ab.close()
            return nc

        es_pb.close()
        es_b.close()
        es_ab.close()

        es_c2 = ExitStack()
        GI = es_c2.enter_context(nc.sbuf_tensor("GI", [128, ICH * T], I16))
        es_cw = ExitStack()
        ec = es_cw.enter_context
        gtl = [ec(nc.sbuf_tensor(f"gtl{i}", [128, 512], F32)) for i in range(2)]
        sigt = [ec(nc.sbuf_tensor(f"sigt{i}", [128, 512], F32)) for i in range(3)]
        g2t = ec(nc.sbuf_tensor("g2t", [128, 512], F32))
        sq16 = ec(nc.sbuf_tensor("sq16", [128, 512], U16))
        sq16b = [ec(nc.sbuf_tensor(f"sq16b{i}", [128, 512], U16)) for i in range(2)]
        g2tb = [ec(nc.sbuf_tensor(f"g2tb{i}", [128, 512], F32)) for i in range(2)]
        oi32b = [ec(nc.sbuf_tensor(f"oi32b{i}", [128, 512], I32)) for i in range(2)]
        utl = [ec(nc.sbuf_tensor(f"utl{i}", [128, 512], F32)) for i in range(2)]
        ui16 = [ec(nc.sbuf_tensor(f"ui16_{i}", [128, 512], I16)) for i in range(2)]
        g2t2 = [ec(nc.sbuf_tensor(f"g2t2{i}", [128, 512], F32)) for i in range(2)]
        oi32 = ec(nc.sbuf_tensor("oi32", [128, 512], I32))
        wdn = ec(nc.sbuf_tensor("wdn", [128, ICH * 1024], BF16))
        x1t = [ec(nc.sbuf_tensor(f"x1t{i}", [128, 1024], F32)) for i in range(2)]
        y2 = ec(nc.sbuf_tensor("y2", [128, 1024], F32))
        y2b = ec(nc.sbuf_tensor("y2b", [128, 1024], F32))
        outt = ec(nc.sbuf_tensor("outt", [128, 1024], F32))

        NT6 = ICH * NTS

        def gi_sl(mmi, dt=None):
            ic, ts_ = mmi // NTS, mmi % NTS
            ap = GI[:, ic * T + ts_ * 512: ic * T + ts_ * 512 + 512]
            return ap if dt is None else ap.bitcast(dt)

        es_pc = ExitStack()
        pX = es_pc.enter_context(nc.psum_tensor("pX", [128, 1024], F32))
        pXb = es_pc.enter_context(nc.psum_tensor("pXb", [128, 1024], F32))
        pXc = es_pc.enter_context(nc.psum_tensor("pXc", [128, 1024], F32))

        _scope_end()
        _scope_begin("ffn2")
        @blk.sync
        def _(sync):
            sync.wait_ge(sSP.h, SP_ALL[0])
            sync.wait_ge(sSP1.h, SP_ALL[1])
            sync.wait_ge(sSPX.h, SP_ALL[2])
            sync.wait_ge(sBC.h, BC2)
            for mmi in range(NT6):
                ic, ts_ = mmi // NTS, mmi % NTS
                if mmi >= 2:
                    sync.wait_ge(sSGA.h, mmi - 1)
                (sGT if mmi % 2 == 0 else sGT1).inc(sync.dma_start(
                    out=gtl[mmi % 2][:, :],
                    in_=gatedram[ic * 128:(ic + 1) * 128, ts_ * 512:ts_ * 512 + 512]), 16)
            for ic in range(ICH):
                sWD.inc(sync.dma_start(out=wdn[:, ic * 1024:(ic + 1) * 1024],
                                       in_=agd_out[ic * 128:(ic + 1) * 128, :]), 16)
        WD_ALL = sWD.n
        V2B = sV2.n
        GPB = sGP.n
        RMB2 = 97

        @blk.scalar
        def _(s):
            s.wait_ge(sBC.h, BC2)
            for mmi in range(NT6):
                s.wait_ge((sGT if mmi % 2 == 0 else sGT1).h, 16 * (mmi // 2 + 1))
                sSGA.inc(s.activation(gi_sl(mmi), gtl[mmi % 2][:, :], AF.Identity,
                                      scale=r_gt[:, :]))
                if mmi >= 3:
                    s.wait_ge(sV2.h, V2B + mmi - 2)
                sSGB.inc(s.activation(sigt[mmi % 3][:, :], gi_sl(mmi), AF.Sigmoid,
                                      scale=s_gt[:, :]))

        @blk.vector
        def _(v):
            v.wait_ge(sBC.h, BC2)
            for mmi in range(NT6):
                v.wait_ge(sSGB.h, mmi + 1)
                if mmi >= 2:
                    v.wait_ge(sGP.h, GPB + mmi - 1)
                sV2.inc(v.tensor_scalar(sq16b[mmi % 2][:, :], sigt[mmi % 3][:, :],
                                        65536.0, 65535.0, ALU.mult, ALU.min))
                if mmi >= 2:
                    sRM.inc(v.tensor_reduce(tredA[:, mmi - 2:mmi - 1],
                                            g2tb[mmi % 2][:, :], AX.X, ALU.max,
                                            apply_absolute_value=True))
            for mmi in (NT6, NT6 + 1):
                v.wait_ge(sGP.h, GPB + mmi - 1)
                sRM.inc(v.tensor_reduce(tredA[:, mmi - 2:mmi - 1],
                                        g2tb[mmi % 2][:, :], AX.X, ALU.max,
                                        apply_absolute_value=True))
            v.wait_ge(sRM.h, sRM.n)
            sRM.inc(v.tensor_reduce(rmax_g2[:, :], tredA[:, :], AX.X, ALU.max))
        RM_G2 = sRM.n
        V2_F2 = sV2.n

        @blk.gpsimd
        def _(g):
            for mmi in range(NT6):
                g.wait_ge(sV2.h, V2B + mmi + 1)
                if mmi >= 2:
                    g.wait_ge(sRM.h, RMB2 + mmi - 1)
                sGP.inc(g.tensor_tensor(g2tb[mmi % 2][:, :], gi_sl(mmi),
                                        sq16b[mmi % 2][:, :], ALU.mult))
        GP_F2 = sGP.n

        _scope_end()
        _scope_begin("ar3")
        @blk.tensor
        def _(t):
            t.wait_ge(sRM.h, RM_G2)
            sPT.inc(t.transpose(pX[0:1, 0:128], rmax_g2[:, :], identf[:, :]))
        PT_A3 = sPT.n

        @blk.vector
        def _(v):
            v.wait_ge(sPT.h, PT_A3)
            fv = mkfence(v, sFV)
            fv(v.tensor_copy(rowt[:, :], pX[0:1, 0:128]))
            sBC.inc(v.tensor_reduce(gsc[0:1, 0:1], rowt[0:1, :], AX.X, ALU.max))
        BC_A3 = sBC.n

        @blk.gpsimd
        def _(g):
            g.wait_ge(sBC.h, BC_A3)
            sGL.inc(g.dma_start(out=ccin[2][0:1, 0:16], in_=gsc[0:1, 0:16]), 16)
            g.wait_ge(sGL.h, sGL.n)
            sAR.inc(g.collective_compute("AllReduce", ALU.max, replica_groups=RG,
                                         ins=[ccin[2][:, :]], outs=[ccout[2][:, :]]))
            g.wait_ge(sAR.h, sAR.n)
            sGL.inc(g.dma_start(out=gsc[0:1, 0:16], in_=ccout[2][0:1, 0:16]), 16)
        GL_A3 = sGL.n

        @blk.tensor
        def _(t):
            t.wait_ge(sGL.h, GL_A3)
            sPT.inc(t.matmul(pX[:, 0:16], onesr[:, :], gsc[0:1, 0:16], start=True, stop=True))
        PT_B3 = sPT.n

        @blk.vector
        def _(v):
            v.wait_ge(sPT.h, PT_B3)
            fv = mkfence(v, sFV)
            fv(v.tensor_copy(bcast[:, :], pX[:, 0:16]))
            fv(v.tensor_tensor(t_c1[:, :], bcast[:, 0:1], sg216[:, :], ALU.mult))
            fv(v.tensor_scalar(s_2[:, :], t_c1[:, :], 1.0 / QMAX, 1e-12, ALU.mult, ALU.max))
            fv(v.reciprocal(k_2[:, :], s_2[:, :]))
            sBC.inc(v.tensor_tensor(k_2[:, :], k_2[:, :], sg216[:, :], ALU.mult))
        BC3 = sBC.n

        if stage == "f3":
            early_out([(sBC, BC3)])
            es_pc.close()
            es_cw.close()
            es_c2.close()
            return nc

        _scope_end()
        _scope_begin("ffn34")
        @blk.sync
        def _(sync):
            sync.wait_ge(sBC.h, BC3)
            for mmi in range(NT6):
                ic, ts_ = mmi // NTS, mmi % NTS
                if mmi >= 2:
                    sync.wait_ge(sSGB.h, NT6 + mmi - 1)
                (sUPL if mmi % 2 == 0 else sUPL1).inc(sync.dma_start(
                    out=utl[mmi % 2][:, :],
                    in_=updram[ic * 128:(ic + 1) * 128, ts_ * 512:ts_ * 512 + 512]), 16)

        @blk.scalar
        def _(s):
            s.wait_ge(sBC.h, BC3)
            for m in range(NT6 + 1):
                if m < NT6:
                    if m >= 3:
                        s.wait_ge(sV2.h, V2_F2 + m - 2)
                    s.activation(sigt[m % 3][:, :], gi_sl(m), AF.Sigmoid, scale=s_gt[:, :])
                    s.wait_ge((sUPL if m % 2 == 0 else sUPL1).h, 16 * (m // 2 + 1))
                    if m >= 2:
                        s.wait_ge(sGP.h, GP_F2 + m - 1)
                    sSGB.inc(s.activation(ui16[m % 2][:, :], utl[m % 2][:, :], AF.Identity,
                                          scale=r_up[:, :]))
                if m >= 1:
                    s.wait_ge(sV2.h, V2_F2 + m)
                    sSGA.inc(s.activation(gi_sl(m - 1), g2t2[(m - 1) % 2][:, :], AF.Identity,
                                          scale=k_2[:, :]))

        @blk.vector
        def _(v):
            v.wait_ge(sBC.h, BC3)
            for m in range(NT6 + 2):
                if m < NT6:
                    v.wait_ge(sSGB.h, NT6 + m + 1)
                    v.tensor_scalar(sq16[:, :], sigt[m % 3][:, :], 65536.0, 65535.0,
                                    ALU.mult, ALU.min)
                    if m >= 2:
                        v.wait_ge(sSGA.h, NT6 + m - 1)
                    sV2.inc(v.tensor_tensor(g2t2[m % 2][:, :], gi_sl(m), sq16[:, :],
                                            ALU.mult))
                if m >= 2:
                    v.wait_ge(sGP.h, GP_F2 + m - 1)
                    sOC.inc(v.tensor_reduce(tredB[:, m - 2:m - 1], g2tb[m % 2][:, :],
                                            AX.X, ALU.max, apply_absolute_value=True))
            v.wait_ge(sOC.h, sOC.n)
            sOC.inc(v.tensor_reduce(rmax_o[:, :], tredB[:, :], AX.X, ALU.max))
        RM_O_OC = sOC.n

        @blk.gpsimd
        def _(g):
            for j in range(NT6):
                g.wait_ge(sSGA.h, NT6 + j + 1)
                g.wait_ge(sSGB.h, NT6 + j + 1)
                if j >= 2:
                    g.wait_ge(sOC.h, j - 1)
                sGP.inc(g.tensor_tensor(g2tb[j % 2][:, :], gi_sl(j),
                                        ui16[j % 2][:, :], ALU.mult))
        GP_M3 = sGP.n

        _scope_end()
        _scope_begin("ar4")
        @blk.tensor
        def _(t):
            t.wait_ge(sOC.h, RM_O_OC)
            sPT.inc(t.transpose(pX[0:1, 0:128], rmax_o[:, :], identf[:, :]))
        PT_A4 = sPT.n

        @blk.vector
        def _(v):
            v.wait_ge(sPT.h, PT_A4)
            fv = mkfence(v, sFV)
            fv(v.tensor_copy(rowt[:, :], pX[0:1, 0:128]))
            sBC.inc(v.tensor_reduce(gsc[0:1, 0:1], rowt[0:1, :], AX.X, ALU.max))
        BC_A4 = sBC.n

        @blk.gpsimd
        def _(g):
            g.wait_ge(sBC.h, BC_A4)
            sGL.inc(g.dma_start(out=ccin[3][0:1, 0:16], in_=gsc[0:1, 0:16]), 16)
            g.wait_ge(sGL.h, sGL.n)
            sAR.inc(g.collective_compute("AllReduce", ALU.max, replica_groups=RG,
                                         ins=[ccin[3][:, :]], outs=[ccout[3][:, :]]))
            g.wait_ge(sAR.h, sAR.n)
            sGL.inc(g.dma_start(out=gsc[0:1, 0:16], in_=ccout[3][0:1, 0:16]), 16)
        GL_A4 = sGL.n

        @blk.tensor
        def _(t):
            t.wait_ge(sGL.h, GL_A4)
            sPT.inc(t.matmul(pX[:, 0:16], onesr[:, :], gsc[0:1, 0:16], start=True, stop=True))
        PT_B4 = sPT.n

        @blk.vector
        def _(v):
            v.wait_ge(sPT.h, PT_B4)
            fv = mkfence(v, sFV)
            fv(v.tensor_copy(bcast[:, :], pX[:, 0:16]))
            fv(v.tensor_tensor(t_c1[:, :], s_2[:, :], s_up[:, :], ALU.mult))
            fv(v.tensor_tensor(k_o[:, :], bcast[:, 0:1], t_c1[:, :], ALU.mult))
            fv(v.tensor_scalar(s_o[:, :], k_o[:, :], 1.0 / QMAX, 1e-12, ALU.mult, ALU.max))
            fv(v.reciprocal(k_o[:, :], s_o[:, :]))
            sBC.inc(v.tensor_tensor(k_o[:, :], k_o[:, :], t_c1[:, :], ALU.mult))
        BC4 = sBC.n

        if stage == "f4":
            early_out([(sBC, BC4)])
            es_pc.close()
            es_cw.close()
            es_c2.close()
            return nc

        _scope_end()
        _scope_begin("ffn5a")
        SGA_5A0 = sSGA.n

        def m5(m):
            ts_, ic = m // ICH, m % ICH
            return ic * NTS + ts_

        @blk.sync
        def _(sync):
            sync.wait_ge(sBC.h, BC4)
            for m in range(NT6):
                ts_, ic = m // ICH, m % ICH
                if m >= 2:
                    sync.wait_ge(sSGB.h, 2 * NT6 + m - 1)
                (sUP50 if m % 2 == 0 else sUP51).inc(sync.dma_start(
                    out=utl[m % 2][:, :],
                    in_=updram[ic * 128:(ic + 1) * 128, ts_ * 512:ts_ * 512 + 512]), 16)

        @blk.scalar
        def _(s):
            s.wait_ge(sBC.h, BC4)
            for m in range(NT6 + 1):
                if m < NT6:
                    s.wait_ge((sUP50 if m % 2 == 0 else sUP51).h, 16 * (m // 2 + 1))
                    if m >= 2:
                        s.wait_ge(sOC.h, RM_O_OC + m - 1)
                    sSGB.inc(s.activation(ui16[m % 2][:, :], utl[m % 2][:, :], AF.Identity,
                                          scale=r_up[:, :]))
                if m >= 1:
                    s.wait_ge(sOC.h, RM_O_OC + m)
                    sSGA.inc(s.activation(oi32b[(m - 1) % 2][:, :], g2t2[(m - 1) % 2][:, :],
                                          AF.Identity, scale=k_o[:, :]))

        @blk.vector
        def _(v):
            v.wait_ge(sBC.h, BC4)
            for m in range(NT6 + 1):
                if m < NT6:
                    v.wait_ge(sSGB.h, 2 * NT6 + m + 1)
                    if m >= 2:
                        v.wait_ge(sSGA.h, SGA_5A0 + m - 1)
                    sOC.inc(v.tensor_tensor(g2t2[m % 2][:, :], gi_sl(m5(m)),
                                            ui16[m % 2][:, :], ALU.mult))
                if m >= 1:
                    v.wait_ge(sSGA.h, SGA_5A0 + m)
                    sQ2s.inc(v.tensor_scalar(gi_sl(m5(m - 1), BF16),
                                             oi32b[(m - 1) % 2][:, :],
                                             s_o[:, :], None, ALU.mult))
        Q2S_ALL = sQ2s.n

        if stage == "f5":
            early_out([(sQ2s, NT6)])
            es_pc.close()
            es_cw.close()
            es_c2.close()
            return nc

        _scope_end()
        _scope_begin("ffn5b")
        @blk.sync
        def _(sync):
            for it in range(NCH + 1):
                if it < NCH:
                    if it >= 2 and stage != "f7":
                        sync.wait_ge(sY2.h, it - 1)
                    (sX1L if it % 2 == 0 else sX1L1).inc(
                        sync.dma_start(out=x1t[it % 2][:, :],
                                       in_=x1dram[:, it * D:(it + 1) * D]), 16)
                if it >= 1 and stage not in ("f7", "f8", "f9"):
                    sync.wait_ge(sLN2.h, it)
                    sOUT.inc(sync.dma_start(out=out_h[(it - 1) * S:it * S, :],
                                            in_=outt[:, :]), 16)
            if stage not in ("f7", "f8", "f9"):
                sync.wait_ge(sOUT.h, sOUT.n)

        @blk.tensor
        def _(t):
            t.wait_ge(sWD.h, WD_ALL)
            for tc in range(NCH):
                t.wait_ge(sQ2s.h, ICH * (tc // 4 + 1))
                if tc >= 3 and stage != "f7":
                    t.wait_ge(sY2.h, tc - 2)
                pp = (pX, pXb, pXc)[tc % 3]
                for half in range(2):
                    for ic in range(ICH):
                        ins = t.matmul(
                            pp[:, half * 512:(half + 1) * 512],
                            GI[:, ic * T + tc * 128: ic * T + (tc + 1) * 128].bitcast(BF16),
                            wdn[:, ic * 1024 + half * 512: ic * 1024 + half * 512 + 512],
                            start=(ic == 0), stop=(ic == ICH - 1))
                sDM.inc(ins)

        if stage == "f8":
            pass
        if stage == "f7":
            early_out([(sDM, NCH), (sX1L, 16 * (NCH // 2)), (sX1L1, 16 * (NCH // 2))])
            es_pc.close()
            es_cw.close()
            es_c2.close()
            return nc

        @blk.vector
        def _(v):
            for tc in range(NCH):
                v.wait_ge(sDM.h, tc + 1)
                v.wait_ge((sX1L if tc % 2 == 0 else sX1L1).h, 16 * (tc // 2 + 1))
                pp = (pX, pXb, pXc)[tc % 3]
                sY2.inc(v.scalar_tensor_tensor(y2[:, :], x1t[tc % 2][:, :], 1.0, pp[:, :],
                                               ALU.mult, ALU.add, accum_out=sums[:, :]))
                if stage == "f8":
                    continue
                if stage in ("f6", "f9", "f10"):
                    if tc >= 1 and stage in ("f6", "f10"):
                        v.wait_ge(sOUT.h, 16 * tc)
                    sLN2.inc(v.tensor_copy(outt[:, :], y2[:, :]))
                    continue
                v.wait_ge(sSQ.h, NCH + tc + 1)
                fv = mkfence(v, sFV)
                v.tensor_scalar(mean[:, :], sums[:, :], 1.0 / D, None, ALU.mult)
                fv(v.tensor_scalar(ex2[:, :], sumsq[:, :], 1.0 / D, None, ALU.mult))
                fv(v.tensor_tensor(t_c1[:, :], mean[:, :], mean[:, :], ALU.mult))
                fv(v.tensor_tensor(varb[:, :], ex2[:, :], t_c1[:, :], ALU.subtract))
                sVR.inc(v.tensor_scalar(varb[:, :], varb[:, :], 1e-5, None, ALU.add))
                v.wait_ge(sSD.h, NCH + tc + 1)
                fv(v.reciprocal(rstd[:, :], sdb[:, :]))
                if tc >= 1:
                    v.wait_ge(sOUT.h, 16 * tc)
                sLN2.inc(v.tensor_scalar(outt[:, :], y2[:, :], mean[:, :], rstd[:, :],
                                         ALU.subtract, ALU.mult))

        if stage == "f10":
            es_pc.close()
            es_cw.close()
            es_c2.close()
            return nc

        if stage == "f9":
            early_out([(sLN2, NCH)])
            es_pc.close()
            es_cw.close()
            es_c2.close()
            return nc

        if stage == "f8":
            early_out([(sY2, NCH)])
            es_pc.close()
            es_cw.close()
            es_c2.close()
            return nc

        if stage not in ("f6", "f9", "f10"):
            @blk.scalar
            def _(s):
                for tc in range(NCH):
                    s.wait_ge(sY2.h, tc + 1)
                    sSQ.inc(s.activation(y2b[:, :], y2[:, :], AF.Square, accum_out=sumsq[:, :]))
                    s.wait_ge(sVR.h, NCH + tc + 1)
                    sSD.inc(s.activation(sdb[:, :], varb[:, :], AF.Sqrt))

        _scope_end()
        es_pc.close()
        es_cw.close()
        es_c2.close()

    return nc


_NC_CACHE = {}


def _get_nc(stage):
    if stage not in _NC_CACHE:
        _NC_CACHE[stage] = build(stage)
    return _NC_CACHE[stage]


def kernel(L0_attn_inputs, L0_attn_nu_log, L0_attn_theta_log, L0_attn_B_re, L0_attn_B_im,
           L0_attn_C_re, L0_attn_C_im, L0_attn_D_skip, L0_ffn_w_gate, L0_ffn_w_up,
           L0_ffn_w_down):
    x = np.ascontiguousarray(np.asarray(L0_attn_inputs, dtype=np.float32))
    nu_log = np.asarray(L0_attn_nu_log, dtype=np.float32)
    th_log = np.asarray(L0_attn_theta_log, dtype=np.float32)
    B_re = np.asarray(L0_attn_B_re, dtype=np.float32)
    B_im = np.asarray(L0_attn_B_im, dtype=np.float32)
    C_re = np.asarray(L0_attn_C_re, dtype=np.float32)
    C_im = np.asarray(L0_attn_C_im, dtype=np.float32)
    D_skip = np.asarray(L0_attn_D_skip, dtype=np.float32)
    w_gate = np.asarray(L0_ffn_w_gate, dtype=np.float32)
    w_up = np.asarray(L0_ffn_w_up, dtype=np.float32)
    w_down = np.asarray(L0_ffn_w_down, dtype=np.float32)

    btre = np.ascontiguousarray(B_re.T)
    btim = np.ascontiguousarray(B_im.T)
    ctre = np.ascontiguousarray(C_re.T)
    ctim = np.ascontiguousarray(C_im.T)
    nucol = np.ascontiguousarray(nu_log.reshape(8, 128).T)
    thcol = np.ascontiguousarray(th_log.reshape(8, 128).T)
    nurow = np.ascontiguousarray(nu_log.reshape(1, H))
    throw = np.ascontiguousarray(th_log.reshape(1, H))
    dskrow = np.ascontiguousarray(D_skip.reshape(1, D))
    identf = np.eye(128, dtype=np.float32)
    identb = np.eye(128, dtype=ml_dtypes.bfloat16)
    ltones = np.triu(np.ones((128, 128), dtype=np.float64)).astype(ml_dtypes.bfloat16)
    iotac = np.arange(128, dtype=np.float32).reshape(128, 1)
    iotacat = np.ascontiguousarray(
        np.broadcast_to(np.tile(np.arange(128, dtype=np.float32), 8), (128, 1024)))
    onesr = np.ones((1, 128), dtype=np.float32)

    nc = _get_nc(_STAGE)
    in_maps = []
    for c in range(8):
        in_maps.append(dict(
            x=np.ascontiguousarray(x[c]),
            btre=btre, btim=btim, ctre=ctre, ctim=ctim,
            nucol=nucol, thcol=thcol, nurow=nurow, throw=throw, dskrow=dskrow,
            wups=np.ascontiguousarray(w_up[c * ISL:(c + 1) * ISL, :]),
            wgts=np.ascontiguousarray(w_gate[c * ISL:(c + 1) * ISL, :]),
            wdns=np.ascontiguousarray(w_down[:, c * ISL:(c + 1) * ISL]),
            identf=identf, identb=identb, ltones=ltones,
            iotac=iotac, iotacat=iotacat, onesr=onesr,
        ))
    res = run_bass_kernel_spmd(nc, in_maps, core_ids=list(range(8)))
    return np.stack([np.asarray(r["out"]) for r in res.results], axis=0)


# version 17 (final)
# speedup vs baseline: 1.0291x; 1.0291x over previous
import numpy as np
import ml_dtypes
from contextlib import ExitStack

import concourse.bass as bass
import concourse.mybir as mybir
from concourse.bass_utils import run_bass_kernel_spmd

F32 = mybir.dt.float32
BF16 = mybir.dt.bfloat16
I16 = mybir.dt.int16
U16 = mybir.dt.uint16
I32 = mybir.dt.int32
AF = mybir.ActivationFunctionType
ALU = mybir.AluOpType
AX = mybir.AxisListType

T, D, H, II = 2048, 1024, 1024, 3072
S = 128
NCH = T // S
DCH = D // 128
HCH = H // 128
ICH = II // 128
ISL = II // 8
TSUB = 512
NTS = T // TSUB
QMAX = 32767.0
TWO_PI = float(2.0 * np.pi)
INV_2PI = float(1.0 / (2.0 * np.pi))

_STAGE = "full"


def mkfence(eng, st):
    def f(instr):
        st.inc(instr)
        eng.wait_ge(st.h, st.n)
        return instr
    return f


class ST:
    def __init__(self, stack, nc, name):
        self.h = stack.enter_context(nc.semaphore(name))
        self.n = 0

    def inc(self, instr, k=1):
        instr.then_inc(self.h, k)
        self.n += k
        return instr


def _ap(x):
    return x if hasattr(x, "unsqueeze") else x[:, :]


def r3(ap, k=128):
    return _ap(ap).rearrange("p (c j) -> p c j", j=k)


def bc3(ap, reps):
    ap = _ap(ap)
    return ap.unsqueeze(2).broadcast_to([ap.shape[0], ap.shape[1], reps])


def build(stage="full"):
    nc = bass.Bass("TRN2", target_bir_lowering=False, detect_race_conditions=False)

    x_h = nc.dram_tensor("x", [T, D], F32, kind="ExternalInput")
    btre_h = nc.dram_tensor("btre", [D, H], F32, kind="ExternalInput")
    btim_h = nc.dram_tensor("btim", [D, H], F32, kind="ExternalInput")
    ctre_h = nc.dram_tensor("ctre", [H, D], F32, kind="ExternalInput")
    ctim_h = nc.dram_tensor("ctim", [H, D], F32, kind="ExternalInput")
    nucol_h = nc.dram_tensor("nucol", [128, 8], F32, kind="ExternalInput")
    thcol_h = nc.dram_tensor("thcol", [128, 8], F32, kind="ExternalInput")
    nurow_h = nc.dram_tensor("nurow", [1, H], F32, kind="ExternalInput")
    throw_h = nc.dram_tensor("throw", [1, H], F32, kind="ExternalInput")
    dskrow_h = nc.dram_tensor("dskrow", [1, D], F32, kind="ExternalInput")
    wup_h = nc.dram_tensor("wups", [ISL, D], F32, kind="ExternalInput")
    wgt_h = nc.dram_tensor("wgts", [ISL, D], F32, kind="ExternalInput")
    wdn_h = nc.dram_tensor("wdns", [D, ISL], F32, kind="ExternalInput")
    identf_h = nc.dram_tensor("identf", [128, 128], F32, kind="ExternalInput")
    identb_h = nc.dram_tensor("identb", [128, 128], BF16, kind="ExternalInput")
    lt_h = nc.dram_tensor("ltones", [128, 128], BF16, kind="ExternalInput")
    iotac_h = nc.dram_tensor("iotac", [128, 1], F32, kind="ExternalInput")
    iotacat_h = nc.dram_tensor("iotacat", [128, 1024], F32, kind="ExternalInput")
    onesr_h = nc.dram_tensor("onesr", [1, 128], F32, kind="ExternalInput")

    out_h = nc.dram_tensor("out", [T, D], F32, kind="ExternalOutput")
    dbg_h = (nc.dram_tensor("dbg", [128, 8192], F32, kind="ExternalOutput")
             if stage == "x1" else None)
    dbgs_h = (nc.dram_tensor("dbgs", [128, 16], F32, kind="ExternalOutput")
              if stage == "x1" else None)

    updram = nc.dram_tensor("updram", [II, T], F32, kind="Internal")
    gatedram = nc.dram_tensor("gatedram", [II, T], F32, kind="Internal")
    x1dram = nc.dram_tensor("x1dram", [128, NCH * D], F32, kind="Internal")
    agu_in = nc.dram_tensor("agu_in", [D, ISL], BF16, kind="Internal")
    agu_out = nc.dram_tensor("agu_out", [8 * D, ISL], BF16, kind="Internal")
    agg_in = nc.dram_tensor("agg_in", [D, ISL], BF16, kind="Internal")
    agg_out = nc.dram_tensor("agg_out", [8 * D, ISL], BF16, kind="Internal")
    agd_in = nc.dram_tensor("agd_in", [ISL, D], BF16, kind="Internal")
    agd_out = nc.dram_tensor("agd_out", [II, D], BF16, kind="Internal")
    ccin = [nc.dram_tensor(f"cc{i}_in", [1, 16], F32, kind="Internal") for i in range(4)]
    ccout = [nc.dram_tensor(f"cc{i}_out", [1, 16], F32, kind="Internal") for i in range(4)]
    RG = [list(range(8))]

    es = ExitStack()
    with es:
        e = es.enter_context

        identf = e(nc.sbuf_tensor("identf_s", [128, 128], F32))
        identb = e(nc.sbuf_tensor("identb_s", [128, 128], BF16))
        ltones = e(nc.sbuf_tensor("lt_s", [128, 128], BF16))
        iotac = e(nc.sbuf_tensor("iotac_s", [128, 1], F32))
        onesr = e(nc.sbuf_tensor("onesr_s", [1, 128], F32))
        nucol = e(nc.sbuf_tensor("nucol_s", [128, 8], F32))
        thcol = e(nc.sbuf_tensor("thcol_s", [128, 8], F32))
        nu_c = e(nc.sbuf_tensor("nu_c", [128, 8], F32))
        nneg_c = e(nc.sbuf_tensor("nneg_c", [128, 8], F32))
        th2p_c = e(nc.sbuf_tensor("th2p_c", [128, 8], F32))
        phre = e(nc.sbuf_tensor("phre", [128, 8], F32))
        phim = e(nc.sbuf_tensor("phim", [128, 8], F32))
        lcre = [e(nc.sbuf_tensor(f"lcre{i}", [128, 8], F32)) for i in range(2)]
        lcim = [e(nc.sbuf_tensor(f"lcim{i}", [128, 8], F32)) for i in range(2)]
        sm8a = e(nc.sbuf_tensor("sm8a", [128, 8], F32))
        sm8b = e(nc.sbuf_tensor("sm8b", [128, 8], F32))
        sm8c = e(nc.sbuf_tensor("sm8c", [128, 8], F32))
        sm8i = e(nc.sbuf_tensor("sm8i", [128, 8], I32))
        sums = e(nc.sbuf_tensor("sums", [128, 1], F32))
        sumsq = e(nc.sbuf_tensor("sumsq", [128, 1], F32))
        mean = e(nc.sbuf_tensor("mean", [128, 1], F32))
        varb = e(nc.sbuf_tensor("varb", [128, 1], F32))
        sdb = e(nc.sbuf_tensor("sdb", [128, 1], F32))
        rstd = e(nc.sbuf_tensor("rstd", [128, 1], F32))
        ex2 = e(nc.sbuf_tensor("ex2", [128, 1], F32))
        rmax_x1 = e(nc.sbuf_tensor("rmax_x1", [128, 1], F32))
        rmax_up = e(nc.sbuf_tensor("rmax_up", [128, 1], F32))
        rmax_gt = e(nc.sbuf_tensor("rmax_gt", [128, 1], F32))
        rmax_g2 = e(nc.sbuf_tensor("rmax_g2", [128, 1], F32))
        rmax_o = e(nc.sbuf_tensor("rmax_o", [128, 1], F32))
        tred = e(nc.sbuf_tensor("tred", [128, 1], F32))
        tredA = e(nc.sbuf_tensor("tredA", [128, 96], F32))
        tredB = e(nc.sbuf_tensor("tredB", [128, 96], F32))
        rowt = e(nc.sbuf_tensor("rowt", [1, 128], F32))
        gsc = e(nc.sbuf_tensor("gsc", [1, 16], F32))
        bcast = e(nc.sbuf_tensor("bcast", [128, 16], F32))
        s_x = e(nc.sbuf_tensor("s_x", [128, 1], F32))
        r_x = e(nc.sbuf_tensor("r_x", [128, 1], F32))
        s_up = e(nc.sbuf_tensor("s_up", [128, 1], F32))
        r_up = e(nc.sbuf_tensor("r_up", [128, 1], F32))
        s_gt = e(nc.sbuf_tensor("s_gt", [128, 1], F32))
        r_gt = e(nc.sbuf_tensor("r_gt", [128, 1], F32))
        sg216 = e(nc.sbuf_tensor("sg216", [128, 1], F32))
        s_2 = e(nc.sbuf_tensor("s_2", [128, 1], F32))
        k_2 = e(nc.sbuf_tensor("k_2", [128, 1], F32))
        s_o = e(nc.sbuf_tensor("s_o", [128, 1], F32))
        k_o = e(nc.sbuf_tensor("k_o", [128, 1], F32))
        t_c1 = e(nc.sbuf_tensor("t_c1", [128, 1], F32))

        sLD = ST(es, nc, "sLD")
        sGL = ST(es, nc, "sGL")
        sX0 = ST(es, nc, "sX0")
        sPV = ST(es, nc, "sPV")
        sPS = ST(es, nc, "sPS")
        sPT = ST(es, nc, "sPT")
        sPD = ST(es, nc, "sPD")
        sWL = ST(es, nc, "sWL")
        sAG = ST(es, nc, "sAG")
        sXT = ST(es, nc, "sXT")
        sXF = ST(es, nc, "sXF")
        sBU = ST(es, nc, "sBU")
        sZ = ST(es, nc, "sZ")
        sHm = ST(es, nc, "sHm")
        sHS = ST(es, nc, "sHS")
        sCP = ST(es, nc, "sCP")
        sY = ST(es, nc, "sY")
        sSQ = ST(es, nc, "sSQ")
        sVR = ST(es, nc, "sVR")
        sSD = ST(es, nc, "sSD")
        sLN = ST(es, nc, "sLN")
        sAR = ST(es, nc, "sAR")
        sBC = ST(es, nc, "sBC")
        sQ1 = ST(es, nc, "sQ1")
        sQ2 = ST(es, nc, "sQ2")
        sQ3 = ST(es, nc, "sQ3")
        sWS = ST(es, nc, "sWS")
        sMM = ST(es, nc, "sMM")
        sEU = ST(es, nc, "sEU")
        sEG = ST(es, nc, "sEG")
        sRM = ST(es, nc, "sRM")
        sSP = ST(es, nc, "sSP")
        sGT = ST(es, nc, "sGT")
        sSGA = ST(es, nc, "sSGA")
        sSGB = ST(es, nc, "sSGB")
        sUPL = ST(es, nc, "sUPL")
        sOC = ST(es, nc, "sOC")
        sWD = ST(es, nc, "sWD")
        sX1L = ST(es, nc, "sX1L")
        sDM = ST(es, nc, "sDM")
        sY2 = ST(es, nc, "sY2")
        sQ2s = ST(es, nc, "sQ2s")
        sV2 = ST(es, nc, "sV2")
        sD2 = ST(es, nc, "sD2")
        sLN2 = ST(es, nc, "sLN2")
        sOUT = ST(es, nc, "sOUT")
        sFV = ST(es, nc, "sFV")
        sZp = ST(es, nc, "sZp")
        sT1 = ST(es, nc, "sT1")
        sGT1 = ST(es, nc, "sGT1")
        sUPL1 = ST(es, nc, "sUPL1")
        sUP50 = ST(es, nc, "sUP50")
        sUP51 = ST(es, nc, "sUP51")
        sWS1 = ST(es, nc, "sWS1")
        sX1L1 = ST(es, nc, "sX1L1")
        sSP1 = ST(es, nc, "sSP1")
        sSPX = ST(es, nc, "sSPX")
        sGP = ST(es, nc, "sGP")
        sHSu = ST(es, nc, "sHSu")

        blk = e(nc.Block())

        _scope_stack = []

        def _scope_begin(nm):
            _scope_stack.append((nm, nc.enter_named_scope(nm, False)[0]))

        def _scope_end():
            nm, sid = _scope_stack.pop()
            nc.leave_named_scope(nm, sid, False)


        es_ab = ExitStack()
        xt = es_ab.enter_context(nc.sbuf_tensor("xt", [128, NCH * D], F32))

        es_a = ExitStack()
        ea = es_a.enter_context
        btre = ea(nc.sbuf_tensor("btre_s", [128, DCH * H], BF16))
        btim = ea(nc.sbuf_tensor("btim_s", [128, DCH * H], BF16))
        ctre = ea(nc.sbuf_tensor("ctre_s", [128, HCH * D], BF16))
        ctimn = ea(nc.sbuf_tensor("ctimn_s", [128, HCH * D], BF16))
        tnre = ea(nc.sbuf_tensor("tnre", [128, H], F32))
        tnim = ea(nc.sbuf_tensor("tnim", [128, H], F32))
        tpre = ea(nc.sbuf_tensor("tpre", [128, H], F32))
        tpim = ea(nc.sbuf_tensor("tpim", [128, H], F32))
        tp127re = ea(nc.sbuf_tensor("tp127re", [128, 8], F32))
        tp127im = ea(nc.sbuf_tensor("tp127im", [128, 8], F32))
        zre = ea(nc.sbuf_tensor("zre", [128, H], BF16))
        zim = ea(nc.sbuf_tensor("zim", [128, H], BF16))
        hsre = ea(nc.sbuf_tensor("hsre", [128, H], BF16))
        hsim = ea(nc.sbuf_tensor("hsim", [128, H], BF16))
        x0f = ea(nc.sbuf_tensor("x0f", [128, D], BF16))
        scr1 = ea(nc.sbuf_tensor("scr1", [128, 1024], F32))
        scr2 = ea(nc.sbuf_tensor("scr2", [128, 1024], F32))
        scr3 = ea(nc.sbuf_tensor("scr3", [128, 1024], F32))
        scr4 = ea(nc.sbuf_tensor("scr4", [128, 1024], F32))
        scri = ea(nc.sbuf_tensor("scri", [128, 1024], I32))
        scr5g = ea(nc.sbuf_tensor("scr5g", [128, 1024], F32))
        scr6g = ea(nc.sbuf_tensor("scr6g", [128, 1024], F32))
        scr7g = ea(nc.sbuf_tensor("scr7g", [128, 1024], F32))
        scr8g = ea(nc.sbuf_tensor("scr8g", [128, 1024], F32))
        t1g = ea(nc.sbuf_tensor("t1g", [128, 1024], F32))
        d1bc = ea(nc.sbuf_tensor("d1bc", [128, D], F32))

        es_a0 = ExitStack()
        e0 = es_a0.enter_context
        iotacat = scr7g
        nbc = scr5g
        t2pbc = scr6g
        rA = scr1
        rB = scr2
        rC = scr3
        rD = scr4
        rE = d1bc
        nurow = rA[0:1, :]
        throw = rB[0:1, :]
        dskrow = rE[0:1, :]
        nrow2 = rC[0:1, :]
        t2prow = rB[0:1, :]
        d1row = rA[0:1, :]
        grow = rD[0:1, :]
        growtmp = rA[0:1, :]
        wld = [scr1] * 2
        wqbf = zre
        wtrs = zim
        bmaxb = hsre[:, :].bitcast(F32)[:, 0:32]
        bsclb = hsre[:, :].bitcast(F32)[:, 32:64]
        binvb = hsre[:, :].bitcast(F32)[:, 64:96]

        es_pa = ExitStack()
        pA = es_pa.enter_context(nc.psum_tensor("pA", [128, 1024], F32))
        pB = es_pa.enter_context(nc.psum_tensor("pB", [128, 1024], F32))
        pHx = es_pa.enter_context(nc.psum_tensor("pHx", [128, 1024], F32))
        pC = es_pa.enter_context(nc.psum_tensor("pCc", [128, 1024], F32))
        pHr = pHx
        pHi = pC

        _scope_begin("prep")
        @blk.sync
        def _(sync):
            for dst, src in [(identf[:, :], identf_h), (identb[:, :], identb_h),
                             (ltones[:, :], lt_h), (iotac[:, :], iotac_h),
                             (onesr[:, :], onesr_h), (nucol[:, :], nucol_h),
                             (thcol[:, :], thcol_h), (nurow, nurow_h),
                             (throw, throw_h), (dskrow, dskrow_h),
                             (iotacat[:, :], iotacat_h)]:
                sLD.inc(sync.dma_start(out=dst, in_=src[:, :]), 16)
            for c in range(NCH):
                sX0.inc(sync.dma_start(out=xt[:, c * D:(c + 1) * D],
                                       in_=x_h[c * S:(c + 1) * S, :]), 16)

        LD0 = sLD.n

        @blk.gpsimd
        def _(g):
            for dch in range(DCH):
                sGL.inc(g.dma_start(out=btre[:, dch * H:(dch + 1) * H],
                                    in_=btre_h[dch * 128:(dch + 1) * 128, :]), 16)
                sGL.inc(g.dma_start(out=btim[:, dch * H:(dch + 1) * H],
                                    in_=btim_h[dch * 128:(dch + 1) * 128, :]), 16)
            for hc in range(HCH):
                sGL.inc(g.dma_start(out=ctre[:, hc * D:(hc + 1) * D],
                                    in_=ctre_h[hc * 128:(hc + 1) * 128, :]), 16)
                sGL.inc(g.dma_start(out=ctimn[:, hc * D:(hc + 1) * D],
                                    in_=ctim_h[hc * 128:(hc + 1) * 128, :]), 16)

        GL0 = sGL.n

        @blk.scalar
        def _(s):
            s.wait_ge(sLD.h, LD0)
            sPS.inc(s.activation(nu_c[:, :], nucol[:, :], AF.Exp))
            sPS.inc(s.activation(sm8a[:, :], thcol[:, :], AF.Exp))
            sPS.inc(s.activation(nrow2, nurow, AF.Exp))
            sPS.inc(s.activation(rD[0:1, :], throw, AF.Exp))
        PS1 = sPS.n

        @blk.vector
        def _(v):
            v.wait_ge(sPS.h, PS1)
            v.memset(gsc[:, :], 0.0)
            sPV.inc(v.tensor_scalar(th2p_c[:, :], sm8a[:, :], INV_2PI, None, ALU.mult))
            sPV.inc(v.tensor_scalar(nneg_c[:, :], nu_c[:, :], -1.0, None, ALU.mult))
            sPV.inc(v.tensor_scalar(t2prow, rD[0:1, :], INV_2PI, None, ALU.mult))
        PV1 = sPV.n

        @blk.scalar
        def _(s):
            s.wait_ge(sPV.h, PV1)
            sPS.inc(s.activation(grow, nrow2, AF.Exp, scale=-2.0))
        PS2 = sPS.n

        @blk.vector
        def _(v):
            v.wait_ge(sPS.h, PS2)
            sPV.inc(v.tensor_scalar(growtmp, grow, -1.0, 1.0 + 1e-5, ALU.mult, ALU.add))
        PV2 = sPV.n

        @blk.scalar
        def _(s):
            s.wait_ge(sPV.h, PV2)
            sPS.inc(s.activation(grow, growtmp, AF.Sqrt))
        PS3 = sPS.n

        @blk.vector
        def _(v):
            v.wait_ge(sPS.h, PS3)
            sPV.inc(v.tensor_scalar(d1row, dskrow, 1.0, None, ALU.add))
        PV2b = sPV.n

        @blk.tensor
        def _(t):
            t.wait_ge(sPV.h, PV2b)
            for ps, row in [(pA, grow), (pB, nrow2), (pHr, t2prow), (pHi, d1row)]:
                t.matmul(ps[:, 0:512], onesr[:, :], row[0:1, 0:512], start=True, stop=True)
                sPT.inc(t.matmul(ps[:, 512:1024], onesr[:, :], row[0:1, 512:1024],
                                 start=True, stop=True))
        PT1 = sPT.n

        @blk.vector
        def _(v):
            v.wait_ge(sPT.h, PT1)
            sPV.inc(v.tensor_copy(scr4[:, :], pA[:, :]))
            sPV.inc(v.tensor_copy(nbc[:, :], pB[:, :]))
            sPV.inc(v.tensor_copy(t2pbc[:, :], pHr[:, :]))
            sPV.inc(v.tensor_copy(d1bc[:, :], pHi[:, :]))
            v.wait_ge(sGL.h, GL0)
            for dch in range(DCH):
                sPV.inc(v.tensor_tensor(btre[:, dch * H:(dch + 1) * H],
                                        btre[:, dch * H:(dch + 1) * H], scr4[:, :], ALU.mult))
                sPV.inc(v.tensor_tensor(btim[:, dch * H:(dch + 1) * H],
                                        btim[:, dch * H:(dch + 1) * H], scr4[:, :], ALU.mult))
            sPV.inc(v.tensor_scalar(ctimn[:, :], ctimn[:, :], -1.0, None, ALU.mult))
        PV3 = sPV.n

        @blk.scalar
        def _(s):
            s.wait_ge(sPV.h, PV3)
            sPS.inc(s.activation(scr2[:, :], t2pbc[:, :], AF.Identity, scale=iotac[:, :]))
            sPS.inc(s.activation(scr3[:, :], nbc[:, :], AF.Exp, scale=iotac[:, :]))
        PS4 = sPS.n

        @blk.vector
        def _(v):
            v.wait_ge(sPS.h, PS4)
            sPV.inc(v.tensor_scalar(scri[:, :], scr2[:, :], 1.0, None, ALU.mult))
            sPV.inc(v.tensor_tensor(scr2[:, :], scr2[:, :], scri[:, :], ALU.subtract))
            sPV.inc(v.tensor_scalar(scri[:, :], scr2[:, :], 0.5, None, ALU.is_ge))
            sPV.inc(v.tensor_tensor(scr2[:, :], scr2[:, :], scri[:, :], ALU.subtract))
            sPV.inc(v.tensor_scalar(scr4[:, :], scr2[:, :], -1.0, 0.25, ALU.mult, ALU.add))
            sPV.inc(v.tensor_scalar(scri[:, :], scr4[:, :], 0.5, None, ALU.is_ge))
            sPV.inc(v.tensor_tensor(scr4[:, :], scr4[:, :], scri[:, :], ALU.subtract))
        PV4 = sPV.n

        @blk.scalar
        def _(s):
            s.wait_ge(sPV.h, PV4)
            sPS.inc(s.activation(nbc[:, :], scr2[:, :], AF.Sin, scale=TWO_PI))
            sPS.inc(s.activation(t2pbc[:, :], scr4[:, :], AF.Sin, scale=TWO_PI))
        PS5 = sPS.n

        @blk.vector
        def _(v):
            v.wait_ge(sPS.h, PS5)
            sPV.inc(v.tensor_tensor(tnre[:, :], scr3[:, :], t2pbc[:, :], ALU.mult))
            sPV.inc(v.scalar_tensor_tensor(tnim[:, :], scr3[:, :], -1.0, nbc[:, :],
                                           ALU.mult, ALU.mult))
            sPV.inc(v.tensor_tensor(r3(scr2), r3(iotacat), bc3(th2p_c, 128), ALU.mult))
            sPV.inc(v.tensor_tensor(r3(scr3), r3(iotacat), bc3(nneg_c, 128), ALU.mult))
        PV5 = sPV.n

        @blk.scalar
        def _(s):
            s.wait_ge(sPV.h, PV5)
            sPS.inc(s.activation(scr3[:, :], scr3[:, :], AF.Exp))
        PS6 = sPS.n

        @blk.vector
        def _(v):
            v.wait_ge(sPS.h, PS6)
            sPV.inc(v.tensor_scalar(scri[:, :], scr2[:, :], 1.0, None, ALU.mult))
            sPV.inc(v.tensor_tensor(scr2[:, :], scr2[:, :], scri[:, :], ALU.subtract))
            sPV.inc(v.tensor_scalar(scri[:, :], scr2[:, :], 0.5, None, ALU.is_ge))
            sPV.inc(v.tensor_tensor(scr2[:, :], scr2[:, :], scri[:, :], ALU.subtract))
            sPV.inc(v.tensor_scalar(scr4[:, :], scr2[:, :], -1.0, 0.25, ALU.mult, ALU.add))
            sPV.inc(v.tensor_scalar(scri[:, :], scr4[:, :], 0.5, None, ALU.is_ge))
            sPV.inc(v.tensor_tensor(scr4[:, :], scr4[:, :], scri[:, :], ALU.subtract))
            fv = mkfence(v, sFV)
            fv(v.tensor_scalar(sm8b[:, :], th2p_c[:, :], 128.0, None, ALU.mult))
            fv(v.tensor_scalar(sm8i[:, :], sm8b[:, :], 1.0, None, ALU.mult))
            fv(v.tensor_tensor(sm8b[:, :], sm8b[:, :], sm8i[:, :], ALU.subtract))
            fv(v.tensor_scalar(sm8i[:, :], sm8b[:, :], 0.5, None, ALU.is_ge))
            fv(v.tensor_tensor(sm8b[:, :], sm8b[:, :], sm8i[:, :], ALU.subtract))
            fv(v.tensor_scalar(sm8c[:, :], sm8b[:, :], -1.0, 0.25, ALU.mult, ALU.add))
            fv(v.tensor_scalar(sm8i[:, :], sm8c[:, :], 0.5, None, ALU.is_ge))
            sPV.inc(v.tensor_tensor(sm8c[:, :], sm8c[:, :], sm8i[:, :], ALU.subtract))
        PV6 = sPV.n

        @blk.scalar
        def _(s):
            s.wait_ge(sPV.h, PV6)
            sPS.inc(s.activation(nbc[:, :], scr2[:, :], AF.Sin, scale=TWO_PI))
            sPS.inc(s.activation(t2pbc[:, :], scr4[:, :], AF.Sin, scale=TWO_PI))
            sPS.inc(s.activation(sm8b[:, :], sm8b[:, :], AF.Sin, scale=TWO_PI))
            sPS.inc(s.activation(sm8c[:, :], sm8c[:, :], AF.Sin, scale=TWO_PI))
            sPS.inc(s.activation(sm8a[:, :], nu_c[:, :], AF.Exp, scale=-128.0))
        PS7 = sPS.n

        @blk.vector
        def _(v):
            v.wait_ge(sPS.h, PS7)
            sPV.inc(v.tensor_tensor(tpre[:, :], scr3[:, :], t2pbc[:, :], ALU.mult))
            sPV.inc(v.tensor_tensor(tpim[:, :], scr3[:, :], nbc[:, :], ALU.mult))
            sPV.inc(v.tensor_tensor(tp127re[:, :],
                                    r3(scr3)[:, :, 127:128].squeeze(2),
                                    r3(t2pbc)[:, :, 127:128].squeeze(2), ALU.mult))
            sPV.inc(v.tensor_tensor(tp127im[:, :],
                                    r3(scr3)[:, :, 127:128].squeeze(2),
                                    r3(nbc)[:, :, 127:128].squeeze(2), ALU.mult))
            sPV.inc(v.tensor_tensor(phre[:, :], sm8a[:, :], sm8c[:, :], ALU.mult))
            sPV.inc(v.tensor_tensor(phim[:, :], sm8a[:, :], sm8b[:, :], ALU.mult))
            sPV.inc(v.memset(lcre[0][:, :], 0.0))
            sPV.inc(v.memset(lcim[0][:, :], 0.0))
        PV7 = sPV.n

        wp_tiles = []
        for ti in range(3):
            wp_tiles.append(("up", ti))
        for ti in range(3):
            wp_tiles.append(("gt", ti))
        for ti in range(8):
            wp_tiles.append(("dn", ti))

        @blk.gpsimd
        def _(g):
            g.wait_ge(sGL.h, GL0)
            for wi, (w, ti) in enumerate(wp_tiles):
                if w == "up":
                    src = wup_h[ti * 128:(ti + 1) * 128, :]
                    dst = wld[wi % 2][:, 0:1024]
                elif w == "gt":
                    src = wgt_h[ti * 128:(ti + 1) * 128, :]
                    dst = wld[wi % 2][:, 0:1024]
                else:
                    src = wdn_h[ti * 128:(ti + 1) * 128, :]
                    dst = wld[wi % 2][:, 0:ISL]
                if wi >= 1:
                    g.wait_ge(sPV.h, PV7 + 2 * wi - 1)
                else:
                    g.wait_ge(sPV.h, PV3)
                sWL.inc(g.dma_start(out=dst, in_=src), 16)

        @blk.vector
        def _(v):
            v.wait_ge(sPV.h, PV7)
            for wi, (w, ti) in enumerate(wp_tiles):
                n = 1024 if w != "dn" else ISL
                nb = n // 32
                buf = wld[wi % 2][:, 0:n]
                v.wait_ge(sWL.h, 16 * (wi + 1))
                bm = bmaxb[:, 0:nb]
                bs = bsclb[:, 0:nb]
                bi = binvb[:, 0:nb]
                fv = mkfence(v, sFV)
                fv(v.tensor_reduce(bm, r3(buf, 32), AX.X, ALU.max,
                                   apply_absolute_value=True))
                fv(v.tensor_scalar(bs, bm, 1.0 / 7.0, 1e-12, ALU.mult, ALU.max))
                fv(v.reciprocal(bi, bs))
                v.tensor_tensor(r3(scri[:, 0:n], 32), r3(buf, 32), bc3(bi, 32), ALU.mult)
                sPV.inc(v.tensor_tensor(r3(wqbf[:, 0:n], 32), r3(scri[:, 0:n], 32),
                                        bc3(bs, 32), ALU.mult))
                v.wait_ge(sPT.h, PT1 + wi + 1)
                sPV.inc(v.tensor_copy(wtrs[:, 0:n], pA[:, :].bitcast(BF16)[:, 0:n]))

        @blk.tensor
        def _(t):
            for wi, (w, ti) in enumerate(wp_tiles):
                n = 1024 if w != "dn" else ISL
                t.wait_ge(sPV.h, PV7 + 2 * wi + 1)
                if wi >= 1:
                    t.wait_ge(sPV.h, PV7 + 2 * wi)
                pab = pA[:, :].bitcast(BF16)
                for j in range(n // 128):
                    ins = t.transpose(pab[:, j * 128:(j + 1) * 128],
                                      wqbf[:, j * 128:(j + 1) * 128], identb[:, :])
                sPT.inc(ins)
        PT_W = sPT.n

        @blk.sync
        def _(sync):
            for wi, (w, ti) in enumerate(wp_tiles):
                n = 1024 if w != "dn" else ISL
                sync.wait_ge(sPV.h, PV7 + 2 * wi + 2)
                if w == "up":
                    dst = agu_in[:, ti * 128:(ti + 1) * 128].rearrange(
                        "(dc p) i -> p dc i", p=128)
                elif w == "gt":
                    dst = agg_in[:, ti * 128:(ti + 1) * 128].rearrange(
                        "(dc p) i -> p dc i", p=128)
                else:
                    dst = agd_in[:, ti * 128:(ti + 1) * 128].rearrange(
                        "(isub p) d2 -> p isub d2", p=128)
                sPD.inc(sync.dma_start(out=dst, in_=r3(wtrs[:, 0:n], 128)), 16)
        PD_W = sPD.n
        PV_W = sPV.n

        @blk.gpsimd
        def _(g):
            g.wait_ge(sPD.h, PD_W)
            sAG.inc(g.collective_compute("AllGather", ALU.bypass, replica_groups=RG,
                                         ins=[agu_in[:, :]], outs=[agu_out[:, :]]))
            sAG.inc(g.collective_compute("AllGather", ALU.bypass, replica_groups=RG,
                                         ins=[agg_in[:, :]], outs=[agg_out[:, :]]))
            sAG.inc(g.collective_compute("AllGather", ALU.bypass, replica_groups=RG,
                                         ins=[agd_in[:, :]], outs=[agd_out[:, :]]))
        AG_W = sAG.n

        _scope_end()
        _scope_begin("attn")
        @blk.tensor
        def _(t):
            t.wait_ge(sPV.h, PV_W)
            t.wait_ge(sX0.h, 16 * NCH)
            for dch in range(DCH):
                ins = t.transpose(pA[:, dch * 128:(dch + 1) * 128],
                                  xt[:, dch * 128:(dch + 1) * 128], identf[:, :])
            sXT.inc(ins)
            t.wait_ge(sXF.h, 1)
            for half in range(2):
                for dch in range(DCH):
                    ins = t.matmul(pA[:, half * 512:(half + 1) * 512],
                                   x0f[:, dch * 128:(dch + 1) * 128],
                                   btre[:, dch * H + half * 512:dch * H + half * 512 + 512],
                                   start=(dch == 0), stop=(dch == DCH - 1))
            for half in range(2):
                for dch in range(DCH):
                    ins = t.matmul(pB[:, half * 512:(half + 1) * 512],
                                   x0f[:, dch * 128:(dch + 1) * 128],
                                   btim[:, dch * H + half * 512:dch * H + half * 512 + 512],
                                   start=(dch == 0), stop=(dch == DCH - 1))
            sBU.inc(ins)
            for c in range(NCH):
                if c >= 1:
                    t.wait_ge(sHS.h, c)
                    t.wait_ge(sY.h, c - 1)
                    for half in range(2):
                        for hc in range(HCH):
                            t.matmul(pC[:, half * 512:(half + 1) * 512],
                                     hsre[:, hc * 128:(hc + 1) * 128],
                                     ctre[:, hc * D + half * 512:hc * D + half * 512 + 512],
                                     start=(hc == 0), stop=False)
                            ins = t.matmul(
                                pC[:, half * 512:(half + 1) * 512],
                                hsim[:, hc * 128:(hc + 1) * 128],
                                ctimn[:, hc * D + half * 512:hc * D + half * 512 + 512],
                                start=False, stop=(hc == HCH - 1))
                    sCP.inc(ins)
                t.wait_ge(sZ.h, c + 1)
                t.wait_ge(sHSu.h, 2 * c)
                for j in range(4):
                    t.matmul(pHx[:, j * 128:(j + 1) * 128],
                             zre[:, j * 128:(j + 1) * 128], ltones[:, :],
                             start=True, stop=True)
                    ins = t.matmul(pHx[:, 512 + j * 128:512 + (j + 1) * 128],
                                   zim[:, j * 128:(j + 1) * 128], ltones[:, :],
                                   start=True, stop=True)
                sHm.inc(ins)
                if c + 1 < NCH:
                    t.wait_ge(sZp.h, 2 * c + 2)
                    for dch in range(DCH):
                        ins = t.transpose(pA[:, dch * 128:(dch + 1) * 128],
                                          xt[:, (c + 1) * D + dch * 128:
                                             (c + 1) * D + (dch + 1) * 128],
                                          identf[:, :])
                    sXT.inc(ins)
                    t.wait_ge(sXF.h, c + 2)
                    for half in range(2):
                        for dch in range(DCH):
                            ins = t.matmul(
                                pA[:, half * 512:(half + 1) * 512],
                                x0f[:, dch * 128:(dch + 1) * 128],
                                btre[:, dch * H + half * 512:dch * H + half * 512 + 512],
                                start=(dch == 0), stop=(dch == DCH - 1))
                    for half in range(2):
                        for dch in range(DCH):
                            ins = t.matmul(
                                pB[:, half * 512:(half + 1) * 512],
                                x0f[:, dch * 128:(dch + 1) * 128],
                                btim[:, dch * H + half * 512:dch * H + half * 512 + 512],
                                start=(dch == 0), stop=(dch == DCH - 1))
                    sBU.inc(ins)
                t.wait_ge(sHSu.h, 2 * c + 1)
                for j in range(4):
                    t.matmul(pHx[:, j * 128:(j + 1) * 128],
                             zre[:, 512 + j * 128:512 + (j + 1) * 128], ltones[:, :],
                             start=True, stop=True)
                    ins = t.matmul(pHx[:, 512 + j * 128:512 + (j + 1) * 128],
                                   zim[:, 512 + j * 128:512 + (j + 1) * 128], ltones[:, :],
                                   start=True, stop=True)
                sHm.inc(ins)
            t.wait_ge(sHS.h, NCH)
            t.wait_ge(sY.h, NCH - 1)
            for half in range(2):
                for hc in range(HCH):
                    t.matmul(pC[:, half * 512:(half + 1) * 512],
                             hsre[:, hc * 128:(hc + 1) * 128],
                             ctre[:, hc * D + half * 512:hc * D + half * 512 + 512],
                             start=(hc == 0), stop=False)
                    ins = t.matmul(pC[:, half * 512:(half + 1) * 512],
                                   hsim[:, hc * 128:(hc + 1) * 128],
                                   ctimn[:, hc * D + half * 512:hc * D + half * 512 + 512],
                                   start=False, stop=(hc == HCH - 1))
            sCP.inc(ins)

        @blk.scalar
        def _(s):
            for c in range(NCH):
                s.wait_ge(sXT.h, c + 1)
                sXF.inc(s.activation(x0f[:, :], pA[:, :], AF.Identity))
                if c >= 1:
                    s.wait_ge(sY.h, c)
                    sSQ.inc(s.activation(scr4[:, :], scr3[:, :], AF.Square,
                                         accum_out=sumsq[:, :]))
                    s.wait_ge(sVR.h, c)
                    sSD.inc(s.activation(sdb[:, :], varb[:, :], AF.Sqrt))
            s.wait_ge(sY.h, NCH)
            sSQ.inc(s.activation(scr4[:, :], scr3[:, :], AF.Square, accum_out=sumsq[:, :]))
            s.wait_ge(sVR.h, NCH)
            sSD.inc(s.activation(sdb[:, :], varb[:, :], AF.Sqrt))

        @blk.gpsimd
        def _(g):
            for c in range(NCH):
                g.wait_ge(sZp.h, 2 * c + 2)
                g.wait_ge(sHm.h, 2 * c)
                g.tensor_tensor(zre[:, :], scr5g[:, :], scr6g[:, :], ALU.subtract)
                sZ.inc(g.tensor_tensor(zim[:, :], scr7g[:, :], scr8g[:, :], ALU.add))
                g.wait_ge(sX0.h, 16 * NCH)
                g.wait_ge(sLN.h, c)
                g.wait_ge(sPV.h, PV3)
                sT1.inc(g.tensor_tensor(t1g[:, :], xt[:, c * D:(c + 1) * D], d1bc[:, :],
                                        ALU.mult))

        @blk.vector
        def _(v):
            v.wait_ge(sBU.h, 1)
            v.tensor_tensor(scr5g[:, :], tnre[:, :], pA[:, :], ALU.mult)
            sZp.inc(v.tensor_tensor(scr6g[:, :], tnim[:, :], pB[:, :], ALU.mult))
            v.tensor_tensor(scr7g[:, :], tnre[:, :], pB[:, :], ALU.mult)
            sZp.inc(v.tensor_tensor(scr8g[:, :], tnim[:, :], pA[:, :], ALU.mult))
            for c in range(NCH):
                par = c % 2
                for r in range(2):
                    o5 = 512 * r
                    v.wait_ge(sHm.h, 2 * c + r + 1)
                    v.tensor_tensor(r3(scr1[:, o5:o5 + 512]), r3(pHx[:, 0:512]),
                                    bc3(lcre[par][:, 4 * r:4 * r + 4], 128), ALU.add)
                    sHSu.inc(v.tensor_tensor(r3(scr2[:, o5:o5 + 512]), r3(pHx[:, 512:1024]),
                                             bc3(lcim[par][:, 4 * r:4 * r + 4], 128), ALU.add))
                v.tensor_tensor(scr3[:, :], tpre[:, :], scr1[:, :], ALU.mult)
                v.tensor_tensor(scr4[:, :], tpim[:, :], scr2[:, :], ALU.mult)
                v.tensor_tensor(hsre[:, :], scr3[:, :], scr4[:, :], ALU.subtract)
                v.tensor_tensor(scr3[:, :], tpim[:, :], scr1[:, :], ALU.mult)
                v.tensor_tensor(scr4[:, :], tpre[:, :], scr2[:, :], ALU.mult)
                v.tensor_tensor(hsim[:, :], scr3[:, :], scr4[:, :], ALU.add)
                u127re = r3(scr1)[:, :, 127:128].squeeze(2)
                u127im = r3(scr2)[:, :, 127:128].squeeze(2)
                fv = mkfence(v, sFV)
                v.tensor_tensor(sm8a[:, :], phre[:, :], u127re, ALU.mult)
                fv(v.tensor_tensor(sm8b[:, :], phim[:, :], u127im, ALU.mult))
                fv(v.tensor_tensor(lcre[1 - par][:, :], sm8a[:, :], sm8b[:, :], ALU.subtract))
                v.tensor_tensor(sm8a[:, :], phim[:, :], u127re, ALU.mult)
                fv(v.tensor_tensor(sm8b[:, :], phre[:, :], u127im, ALU.mult))
                sHS.inc(v.tensor_tensor(lcim[1 - par][:, :], sm8a[:, :], sm8b[:, :], ALU.add))
                if c + 1 < NCH:
                    v.wait_ge(sBU.h, c + 2)
                    v.wait_ge(sZ.h, c + 1)
                    v.tensor_tensor(scr5g[:, :], tnre[:, :], pA[:, :], ALU.mult)
                    sZp.inc(v.tensor_tensor(scr6g[:, :], tnim[:, :], pB[:, :], ALU.mult))
                    v.tensor_tensor(scr7g[:, :], tnre[:, :], pB[:, :], ALU.mult)
                    sZp.inc(v.tensor_tensor(scr8g[:, :], tnim[:, :], pA[:, :], ALU.mult))
                v.wait_ge(sCP.h, c + 1)
                v.wait_ge(sT1.h, c + 1)
                sY.inc(v.scalar_tensor_tensor(scr3[:, :], t1g[:, :], 1.0, pC[:, :],
                                              ALU.mult, ALU.add, accum_out=sums[:, :]))
                v.wait_ge(sSQ.h, c + 1)
                fv = mkfence(v, sFV)
                v.tensor_scalar(mean[:, :], sums[:, :], 1.0 / D, None, ALU.mult)
                fv(v.tensor_scalar(ex2[:, :], sumsq[:, :], 1.0 / D, None, ALU.mult))
                fv(v.tensor_tensor(t_c1[:, :], mean[:, :], mean[:, :], ALU.mult))
                fv(v.tensor_tensor(varb[:, :], ex2[:, :], t_c1[:, :], ALU.subtract))
                sVR.inc(v.tensor_scalar(varb[:, :], varb[:, :], 1e-5, None, ALU.add))
                v.wait_ge(sSD.h, c + 1)
                fv(v.reciprocal(rstd[:, :], sdb[:, :]))
                v.tensor_scalar(xt[:, c * D:(c + 1) * D], scr3[:, :], mean[:, :], rstd[:, :],
                                ALU.subtract, ALU.mult)
                fv(v.tensor_reduce(tred[:, :], xt[:, c * D:(c + 1) * D], AX.X, ALU.max,
                                   apply_absolute_value=True))
                if c == 0:
                    sLN.inc(v.tensor_copy(rmax_x1[:, :], tred[:, :]))
                else:
                    sLN.inc(v.tensor_tensor(rmax_x1[:, :], rmax_x1[:, :], tred[:, :], ALU.max))

        _scope_end()
        _scope_begin("ar1")
        @blk.tensor
        def _(t):
            t.wait_ge(sLN.h, NCH)
            sPT.inc(t.transpose(pHr[0:1, 0:128], rmax_x1[:, :], identf[:, :]))
        PT_A1 = sPT.n

        @blk.vector
        def _(v):
            v.wait_ge(sPT.h, PT_A1)
            fv = mkfence(v, sFV)
            fv(v.tensor_copy(rowt[:, :], pHr[0:1, 0:128]))
            sLN.inc(v.tensor_reduce(gsc[0:1, 0:1], rowt[0:1, :], AX.X, ALU.max))
        LN_A1 = sLN.n

        @blk.gpsimd
        def _(g):
            g.wait_ge(sAG.h, AG_W)
            g.wait_ge(sLN.h, LN_A1)
            sGL.inc(g.dma_start(out=ccin[0][0:1, 0:16], in_=gsc[0:1, 0:16]), 16)
            g.wait_ge(sGL.h, sGL.n)
            sAR.inc(g.collective_compute("AllReduce", ALU.max, replica_groups=RG,
                                         ins=[ccin[0][:, :]], outs=[ccout[0][:, :]]))
            g.wait_ge(sAR.h, sAR.n)
            sGL.inc(g.dma_start(out=gsc[0:1, 0:16], in_=ccout[0][0:1, 0:16]), 16)
        GL_A1 = sGL.n

        @blk.tensor
        def _(t):
            t.wait_ge(sGL.h, GL_A1)
            sPT.inc(t.matmul(pHi[:, 0:16], onesr[:, :], gsc[0:1, 0:16], start=True, stop=True))
        PT_B1 = sPT.n

        @blk.vector
        def _(v):
            v.wait_ge(sPT.h, PT_B1)
            fv = mkfence(v, sFV)
            fv(v.tensor_copy(bcast[:, :], pHi[:, 0:16]))
            fv(v.tensor_scalar(s_x[:, :], bcast[:, 0:1], 1.0 / QMAX, 1e-12, ALU.mult, ALU.max))
            sBC.inc(v.reciprocal(r_x[:, :], s_x[:, :]))
        BC1 = sBC.n

        es_a0.close()
        es_pa.close()

        if stage == "x1":
            @blk.sync
            def _(sync):
                sync.wait_ge(sBC.h, BC1)
                for c in range(NCH):
                    sOUT.inc(sync.dma_start(out=out_h[c * S:(c + 1) * S, :],
                                            in_=xt[:, c * D:(c + 1) * D]), 16)
                sync.wait_ge(sOUT.h, sOUT.n)
            es_a.close()
            es_ab.close()
            return nc

        es_a.close()

        es_b = ExitStack()
        eb = es_b.enter_context
        xqf = eb(nc.sbuf_tensor("xqf", [128, DCH * T], BF16))
        qint = eb(nc.sbuf_tensor("qint", [128, 1024], I32))
        xqt = eb(nc.sbuf_tensor("xqt", [128, 1024], BF16))
        stup = [eb(nc.sbuf_tensor(f"stup{i}", [128, 1024], BF16)) for i in range(2)]
        stgt = [eb(nc.sbuf_tensor(f"stgt{i}", [128, 1024], BF16)) for i in range(2)]
        stagU = [eb(nc.sbuf_tensor(f"stagU{i}", [128, 512], F32)) for i in range(2)]
        stagG = [eb(nc.sbuf_tensor(f"stagG{i}", [128, 512], F32)) for i in range(2)]

        es_pb = ExitStack()
        pT1 = es_pb.enter_context(nc.psum_tensor("pT1", [128, 1024], F32))
        pU = [es_pb.enter_context(nc.psum_tensor(f"pU{i}", [128, 512], F32)) for i in range(3)]
        pG = [es_pb.enter_context(nc.psum_tensor(f"pG{i}", [128, 512], F32)) for i in range(3)]

        _scope_end()
        _scope_begin("ffn0")
        @blk.vector
        def _(v):
            v.wait_ge(sBC.h, BC1)
            for c in range(NCH):
                v.tensor_scalar(qint[:, :], xt[:, c * D:(c + 1) * D], r_x[:, :], None, ALU.mult)
                sQ1.inc(v.tensor_scalar(xqt[:, :], qint[:, :], s_x[:, :], None, ALU.mult))
                v.wait_ge(sQ2.h, c + 1)
                sQ3.inc(v.tensor_copy(
                    xqf[:, :].rearrange("p (dc t) -> p dc t", t=T)[:, :, c * 128:(c + 1) * 128],
                    r3(pT1[:, :].bitcast(BF16)[:, 0:1024])))

        @blk.tensor
        def _(t):
            for c in range(NCH):
                t.wait_ge(sQ1.h, c + 1)
                if c >= 1:
                    t.wait_ge(sQ3.h, c)
                pqb = pT1[:, :].bitcast(BF16)
                for dch in range(DCH):
                    ins = t.transpose(pqb[:, dch * 128:(dch + 1) * 128],
                                      xqt[:, dch * 128:(dch + 1) * 128], identb[:, :])
                sQ2.inc(ins)

        @blk.sync
        def _(sync):
            sync.wait_ge(sQ3.h, NCH)
            for c in range(NCH):
                sSPX.inc(sync.dma_start(out=x1dram[:, c * D:(c + 1) * D],
                                        in_=xt[:, c * D:(c + 1) * D]), 16)
        SP_X1 = sSPX.n

        _scope_end()
        _scope_begin("ffn1")
        @blk.gpsimd
        def _(g):
            for ic in range(ICH):
                blkid, col = ic // 3, (ic % 3) * 128
                if ic >= 2:
                    g.wait_ge(sMM.h, (ic - 1) * NTS * 2)
                src_u = agu_out[blkid * D:(blkid + 1) * D, col:col + 128].rearrange(
                    "(dc p) i -> p dc i", p=128)
                src_g = agg_out[blkid * D:(blkid + 1) * D, col:col + 128].rearrange(
                    "(dc p) i -> p dc i", p=128)
                sem = sWS if ic % 2 == 0 else sWS1
                sem.inc(g.dma_start(out=r3(stup[ic % 2], 128), in_=src_u), 16)
                sem.inc(g.dma_start(out=r3(stgt[ic % 2], 128), in_=src_g), 16)

        @blk.tensor
        def _(t):
            t.wait_ge(sQ3.h, NCH)
            for ic in range(ICH):
                sem = sWS if ic % 2 == 0 else sWS1
                t.wait_ge(sem.h, 32 * (ic // 2 + 1))
                for ts_ in range(4):
                    mmi = ic * NTS + ts_
                    par = mmi % 3
                    if mmi >= 3:
                        t.wait_ge(sEU.h, mmi - 2)
                        t.wait_ge(sEG.h, mmi - 2)
                    for dch in range(DCH):
                        t.matmul(pU[par][:, :], stup[ic % 2][:, dch * 128:(dch + 1) * 128],
                                 xqf[:, dch * T + ts_ * 512:dch * T + ts_ * 512 + 512],
                                 start=(dch == 0), stop=(dch == DCH - 1))
                    for dch in range(DCH):
                        ins = t.matmul(pG[par][:, :], stgt[ic % 2][:, dch * 128:(dch + 1) * 128],
                                       xqf[:, dch * T + ts_ * 512:dch * T + ts_ * 512 + 512],
                                       start=(dch == 0), stop=(dch == DCH - 1))
                    sMM.inc(ins, 2)

        @blk.scalar
        def _(s):
            for mmi in range(ICH * NTS):
                s.wait_ge(sMM.h, 2 * (mmi + 1))
                if mmi >= 2:
                    s.wait_ge((sSP if mmi % 2 == 0 else sSP1).h, 32 * (mmi // 2))
                sEU.inc(s.activation(stagU[mmi % 2][:, :], pU[mmi % 3][:, :], AF.Identity))

        @blk.vector
        def _(v):
            for mmi in range(ICH * NTS):
                v.wait_ge(sMM.h, 2 * (mmi + 1))
                if mmi >= 2:
                    v.wait_ge((sSP if mmi % 2 == 0 else sSP1).h, 32 * (mmi // 2))
                sEG.inc(v.tensor_copy(stagG[mmi % 2][:, :], pG[mmi % 3][:, :]))
                v.tensor_reduce(tredB[:, mmi:mmi + 1], stagG[mmi % 2][:, :], AX.X, ALU.max,
                                apply_absolute_value=True)
                v.wait_ge(sEU.h, mmi + 1)
                sRM.inc(v.tensor_reduce(tredA[:, mmi:mmi + 1], stagU[mmi % 2][:, :], AX.X,
                                        ALU.max, apply_absolute_value=True))

        @blk.sync
        def _(sync):
            for mmi in range(ICH * NTS):
                ic, ts_ = mmi // NTS, mmi % NTS
                par = mmi % 2
                sem = sSP if par == 0 else sSP1
                sync.wait_ge(sEU.h, mmi + 1)
                sem.inc(sync.dma_start(
                    out=updram[ic * 128:(ic + 1) * 128, ts_ * 512:ts_ * 512 + 512],
                    in_=stagU[par][:, :]), 16)
                sync.wait_ge(sRM.h, mmi + 1)
                sem.inc(sync.dma_start(
                    out=gatedram[ic * 128:(ic + 1) * 128, ts_ * 512:ts_ * 512 + 512],
                    in_=stagG[par][:, :]), 16)
        SP_ALL = (sSP.n, sSP1.n, sSPX.n)

        _scope_end()
        _scope_begin("ar2")
        @blk.vector
        def _(v):
            v.wait_ge(sRM.h, ICH * NTS)
            fv = mkfence(v, sFV)
            fv(v.tensor_reduce(rmax_up[:, :], tredA[:, :], AX.X, ALU.max))
            sRM.inc(v.tensor_reduce(rmax_gt[:, :], tredB[:, :], AX.X, ALU.max))

        @blk.tensor
        def _(t):
            t.wait_ge(sRM.h, ICH * NTS + 1)
            t.transpose(pT1[0:1, 0:128], rmax_up[:, :], identf[:, :])
            sPT.inc(t.transpose(pT1[0:1, 128:256], rmax_gt[:, :], identf[:, :]))
        PT_A2 = sPT.n

        @blk.vector
        def _(v):
            v.wait_ge(sPT.h, PT_A2)
            fv = mkfence(v, sFV)
            fv(v.tensor_copy(rowt[:, :], pT1[0:1, 0:128]))
            fv(v.tensor_reduce(gsc[0:1, 0:1], rowt[0:1, :], AX.X, ALU.max))
            fv(v.tensor_copy(rowt[:, :], pT1[0:1, 128:256]))
            sBC.inc(v.tensor_reduce(gsc[0:1, 1:2], rowt[0:1, :], AX.X, ALU.max))
        BC_A2 = sBC.n

        @blk.gpsimd
        def _(g):
            g.wait_ge(sBC.h, BC_A2)
            sGL.inc(g.dma_start(out=ccin[1][0:1, 0:16], in_=gsc[0:1, 0:16]), 16)
            g.wait_ge(sGL.h, sGL.n)
            sAR.inc(g.collective_compute("AllReduce", ALU.max, replica_groups=RG,
                                         ins=[ccin[1][:, :]], outs=[ccout[1][:, :]]))
            g.wait_ge(sAR.h, sAR.n)
            sGL.inc(g.dma_start(out=gsc[0:1, 0:16], in_=ccout[1][0:1, 0:16]), 16)
        GL_A2 = sGL.n

        @blk.tensor
        def _(t):
            t.wait_ge(sGL.h, GL_A2)
            sPT.inc(t.matmul(pT1[:, 0:16], onesr[:, :], gsc[0:1, 0:16], start=True, stop=True))
        PT_B2 = sPT.n

        @blk.vector
        def _(v):
            v.wait_ge(sPT.h, PT_B2)
            fv = mkfence(v, sFV)
            fv(v.tensor_copy(bcast[:, :], pT1[:, 0:16]))
            fv(v.tensor_scalar(s_up[:, :], bcast[:, 0:1], 1.0 / QMAX, 1e-12, ALU.mult, ALU.max))
            v.reciprocal(r_up[:, :], s_up[:, :])
            fv(v.tensor_scalar(s_gt[:, :], bcast[:, 1:2], 1.0 / QMAX, 1e-12, ALU.mult, ALU.max))
            fv(v.reciprocal(r_gt[:, :], s_gt[:, :]))
            sBC.inc(v.tensor_scalar(sg216[:, :], s_gt[:, :], 1.0 / 65536.0, None, ALU.mult))
        BC2 = sBC.n


        def early_out(gates):
            @blk.sync
            def _(sync):
                for sem, val in gates:
                    sync.wait_ge(sem.h, val)
                for c in range(NCH):
                    sOUT.inc(sync.dma_start(out=out_h[c * S:(c + 1) * S, :],
                                            in_=x1dram[:, c * D:(c + 1) * D]), 16)
                sync.wait_ge(sOUT.h, sOUT.n)

        if stage == "f2":
            early_out([(sBC, BC2), (sSP, SP_ALL)])
            es_pb.close()
            es_b.close()
            es_ab.close()
            return nc

        es_pb.close()
        es_b.close()
        es_ab.close()

        es_c2 = ExitStack()
        GI = es_c2.enter_context(nc.sbuf_tensor("GI", [128, ICH * T], I16))
        es_cw = ExitStack()
        ec = es_cw.enter_context
        gtl = [ec(nc.sbuf_tensor(f"gtl{i}", [128, 512], F32)) for i in range(4)]
        sigt = [ec(nc.sbuf_tensor(f"sigt{i}", [128, 512], F32)) for i in range(3)]
        g2t = ec(nc.sbuf_tensor("g2t", [128, 512], F32))
        sq16 = ec(nc.sbuf_tensor("sq16", [128, 512], U16))
        sq16b = [ec(nc.sbuf_tensor(f"sq16b{i}", [128, 512], U16)) for i in range(2)]
        g2tb = [ec(nc.sbuf_tensor(f"g2tb{i}", [128, 512], F32)) for i in range(2)]
        oi32b = [ec(nc.sbuf_tensor(f"oi32b{i}", [128, 512], I32)) for i in range(2)]
        utl = [ec(nc.sbuf_tensor(f"utl{i}", [128, 512], F32)) for i in range(2)]
        ui16 = [ec(nc.sbuf_tensor(f"ui16_{i}", [128, 512], I16)) for i in range(2)]
        g2t2 = [ec(nc.sbuf_tensor(f"g2t2{i}", [128, 512], F32)) for i in range(2)]
        oi32 = ec(nc.sbuf_tensor("oi32", [128, 512], I32))
        wdn = ec(nc.sbuf_tensor("wdn", [128, ICH * 1024], BF16))
        x1t = [ec(nc.sbuf_tensor(f"x1t{i}", [128, 1024], F32)) for i in range(2)]
        y2 = ec(nc.sbuf_tensor("y2", [128, 1024], F32))
        y2b = ec(nc.sbuf_tensor("y2b", [128, 1024], F32))
        outt = ec(nc.sbuf_tensor("outt", [128, 1024], F32))

        NT6 = ICH * NTS

        def gi_sl(mmi, dt=None):
            ic, ts_ = mmi // NTS, mmi % NTS
            ap = GI[:, ic * T + ts_ * 512: ic * T + ts_ * 512 + 512]
            return ap if dt is None else ap.bitcast(dt)

        es_pc = ExitStack()
        pX = es_pc.enter_context(nc.psum_tensor("pX", [128, 1024], F32))
        pXb = es_pc.enter_context(nc.psum_tensor("pXb", [128, 1024], F32))

        _scope_end()
        _scope_begin("ffn2")
        GTSEM = (sGT, sGT1, sLD, sX0)
        GTB = tuple(x.n for x in GTSEM)

        @blk.sync
        def _(sync):
            sync.wait_ge(sSP.h, SP_ALL[0])
            sync.wait_ge(sSP1.h, SP_ALL[1])
            sync.wait_ge(sSPX.h, SP_ALL[2])
            sync.wait_ge(sBC.h, BC2)
            for mmi in range(NT6):
                ic, ts_ = mmi // NTS, mmi % NTS
                if mmi >= 4:
                    sync.wait_ge(sSGA.h, mmi - 3)
                GTSEM[mmi % 4].inc(sync.dma_start(
                    out=gtl[mmi % 4][:, :],
                    in_=gatedram[ic * 128:(ic + 1) * 128, ts_ * 512:ts_ * 512 + 512]), 16)
            for ic in range(ICH):
                sWD.inc(sync.dma_start(out=wdn[:, ic * 1024:(ic + 1) * 1024],
                                       in_=agd_out[ic * 128:(ic + 1) * 128, :]), 16)
        WD_ALL = sWD.n
        V2B = sV2.n
        GPB = sGP.n
        RMB2 = 97

        @blk.scalar
        def _(s):
            s.wait_ge(sBC.h, BC2)
            for mmi in range(NT6):
                s.wait_ge(GTSEM[mmi % 4].h, GTB[mmi % 4] + 16 * (mmi // 4 + 1))
                sSGA.inc(s.activation(gi_sl(mmi), gtl[mmi % 4][:, :], AF.Identity,
                                      scale=r_gt[:, :]))
                if mmi >= 3:
                    s.wait_ge(sV2.h, V2B + mmi - 2)
                sSGB.inc(s.activation(sigt[mmi % 3][:, :], gi_sl(mmi), AF.Sigmoid,
                                      scale=s_gt[:, :]))

        @blk.vector
        def _(v):
            v.wait_ge(sBC.h, BC2)
            for mmi in range(NT6):
                v.wait_ge(sSGB.h, mmi + 1)
                if mmi >= 2:
                    v.wait_ge(sGP.h, GPB + mmi - 1)
                sV2.inc(v.tensor_scalar(sq16b[mmi % 2][:, :], sigt[mmi % 3][:, :],
                                        65536.0, 65535.0, ALU.mult, ALU.min))
                if mmi >= 2:
                    sRM.inc(v.tensor_reduce(tredA[:, mmi - 2:mmi - 1],
                                            g2tb[mmi % 2][:, :], AX.X, ALU.max,
                                            apply_absolute_value=True))
            for mmi in (NT6, NT6 + 1):
                v.wait_ge(sGP.h, GPB + mmi - 1)
                sRM.inc(v.tensor_reduce(tredA[:, mmi - 2:mmi - 1],
                                        g2tb[mmi % 2][:, :], AX.X, ALU.max,
                                        apply_absolute_value=True))
            v.wait_ge(sRM.h, sRM.n)
            sRM.inc(v.tensor_reduce(rmax_g2[:, :], tredA[:, :], AX.X, ALU.max))
        RM_G2 = sRM.n
        V2_F2 = sV2.n

        @blk.gpsimd
        def _(g):
            for mmi in range(NT6):
                g.wait_ge(sV2.h, V2B + mmi + 1)
                if mmi >= 2:
                    g.wait_ge(sRM.h, RMB2 + mmi - 1)
                sGP.inc(g.tensor_tensor(g2tb[mmi % 2][:, :], gi_sl(mmi),
                                        sq16b[mmi % 2][:, :], ALU.mult))
        GP_F2 = sGP.n

        _scope_end()
        _scope_begin("ar3")
        @blk.tensor
        def _(t):
            t.wait_ge(sRM.h, RM_G2)
            sPT.inc(t.transpose(pX[0:1, 0:128], rmax_g2[:, :], identf[:, :]))
        PT_A3 = sPT.n

        @blk.vector
        def _(v):
            v.wait_ge(sPT.h, PT_A3)
            fv = mkfence(v, sFV)
            fv(v.tensor_copy(rowt[:, :], pX[0:1, 0:128]))
            sBC.inc(v.tensor_reduce(gsc[0:1, 0:1], rowt[0:1, :], AX.X, ALU.max))
        BC_A3 = sBC.n

        @blk.gpsimd
        def _(g):
            g.wait_ge(sBC.h, BC_A3)
            sGL.inc(g.dma_start(out=ccin[2][0:1, 0:16], in_=gsc[0:1, 0:16]), 16)
            g.wait_ge(sGL.h, sGL.n)
            sAR.inc(g.collective_compute("AllReduce", ALU.max, replica_groups=RG,
                                         ins=[ccin[2][:, :]], outs=[ccout[2][:, :]]))
            g.wait_ge(sAR.h, sAR.n)
            sGL.inc(g.dma_start(out=gsc[0:1, 0:16], in_=ccout[2][0:1, 0:16]), 16)
        GL_A3 = sGL.n

        @blk.tensor
        def _(t):
            t.wait_ge(sGL.h, GL_A3)
            sPT.inc(t.matmul(pX[:, 0:16], onesr[:, :], gsc[0:1, 0:16], start=True, stop=True))
        PT_B3 = sPT.n

        @blk.vector
        def _(v):
            v.wait_ge(sPT.h, PT_B3)
            fv = mkfence(v, sFV)
            fv(v.tensor_copy(bcast[:, :], pX[:, 0:16]))
            fv(v.tensor_tensor(t_c1[:, :], bcast[:, 0:1], sg216[:, :], ALU.mult))
            fv(v.tensor_scalar(s_2[:, :], t_c1[:, :], 1.0 / QMAX, 1e-12, ALU.mult, ALU.max))
            fv(v.reciprocal(k_2[:, :], s_2[:, :]))
            sBC.inc(v.tensor_tensor(k_2[:, :], k_2[:, :], sg216[:, :], ALU.mult))
        BC3 = sBC.n

        if stage == "f3":
            early_out([(sBC, BC3)])
            es_pc.close()
            es_cw.close()
            es_c2.close()
            return nc

        _scope_end()
        _scope_begin("ffn34")
        @blk.sync
        def _(sync):
            sync.wait_ge(sBC.h, BC3)
            for mmi in range(NT6):
                ic, ts_ = mmi // NTS, mmi % NTS
                if mmi >= 2:
                    sync.wait_ge(sSGB.h, NT6 + mmi - 1)
                (sUPL if mmi % 2 == 0 else sUPL1).inc(sync.dma_start(
                    out=utl[mmi % 2][:, :],
                    in_=updram[ic * 128:(ic + 1) * 128, ts_ * 512:ts_ * 512 + 512]), 16)

        @blk.scalar
        def _(s):
            s.wait_ge(sBC.h, BC3)
            for m in range(NT6 + 1):
                if m < NT6:
                    if m >= 3:
                        s.wait_ge(sV2.h, V2_F2 + m - 2)
                    s.activation(sigt[m % 3][:, :], gi_sl(m), AF.Sigmoid, scale=s_gt[:, :])
                    s.wait_ge((sUPL if m % 2 == 0 else sUPL1).h, 16 * (m // 2 + 1))
                    if m >= 2:
                        s.wait_ge(sGP.h, GP_F2 + m - 1)
                    sSGB.inc(s.activation(ui16[m % 2][:, :], utl[m % 2][:, :], AF.Identity,
                                          scale=r_up[:, :]))
                if m >= 1:
                    s.wait_ge(sV2.h, V2_F2 + m)
                    sSGA.inc(s.activation(gi_sl(m - 1), g2t2[(m - 1) % 2][:, :], AF.Identity,
                                          scale=k_2[:, :]))

        @blk.vector
        def _(v):
            v.wait_ge(sBC.h, BC3)
            for m in range(NT6 + 2):
                if m < NT6:
                    v.wait_ge(sSGB.h, NT6 + m + 1)
                    v.tensor_scalar(sq16[:, :], sigt[m % 3][:, :], 65536.0, 65535.0,
                                    ALU.mult, ALU.min)
                    if m >= 2:
                        v.wait_ge(sSGA.h, NT6 + m - 1)
                    sV2.inc(v.tensor_tensor(g2t2[m % 2][:, :], gi_sl(m), sq16[:, :],
                                            ALU.mult))
                if m >= 2:
                    v.wait_ge(sGP.h, GP_F2 + m - 1)
                    sOC.inc(v.tensor_reduce(tredB[:, m - 2:m - 1], g2tb[m % 2][:, :],
                                            AX.X, ALU.max, apply_absolute_value=True))
            v.wait_ge(sOC.h, sOC.n)
            sOC.inc(v.tensor_reduce(rmax_o[:, :], tredB[:, :], AX.X, ALU.max))
        RM_O_OC = sOC.n

        @blk.gpsimd
        def _(g):
            for j in range(NT6):
                g.wait_ge(sSGA.h, NT6 + j + 1)
                g.wait_ge(sSGB.h, NT6 + j + 1)
                if j >= 2:
                    g.wait_ge(sOC.h, j - 1)
                sGP.inc(g.tensor_tensor(g2tb[j % 2][:, :], gi_sl(j),
                                        ui16[j % 2][:, :], ALU.mult))
        GP_M3 = sGP.n

        _scope_end()
        _scope_begin("ar4")
        @blk.tensor
        def _(t):
            t.wait_ge(sOC.h, RM_O_OC)
            sPT.inc(t.transpose(pX[0:1, 0:128], rmax_o[:, :], identf[:, :]))
        PT_A4 = sPT.n

        @blk.vector
        def _(v):
            v.wait_ge(sPT.h, PT_A4)
            fv = mkfence(v, sFV)
            fv(v.tensor_copy(rowt[:, :], pX[0:1, 0:128]))
            sBC.inc(v.tensor_reduce(gsc[0:1, 0:1], rowt[0:1, :], AX.X, ALU.max))
        BC_A4 = sBC.n

        @blk.gpsimd
        def _(g):
            g.wait_ge(sBC.h, BC_A4)
            sGL.inc(g.dma_start(out=ccin[3][0:1, 0:16], in_=gsc[0:1, 0:16]), 16)
            g.wait_ge(sGL.h, sGL.n)
            sAR.inc(g.collective_compute("AllReduce", ALU.max, replica_groups=RG,
                                         ins=[ccin[3][:, :]], outs=[ccout[3][:, :]]))
            g.wait_ge(sAR.h, sAR.n)
            sGL.inc(g.dma_start(out=gsc[0:1, 0:16], in_=ccout[3][0:1, 0:16]), 16)
        GL_A4 = sGL.n

        @blk.tensor
        def _(t):
            t.wait_ge(sGL.h, GL_A4)
            sPT.inc(t.matmul(pX[:, 0:16], onesr[:, :], gsc[0:1, 0:16], start=True, stop=True))
        PT_B4 = sPT.n

        @blk.vector
        def _(v):
            v.wait_ge(sPT.h, PT_B4)
            fv = mkfence(v, sFV)
            fv(v.tensor_copy(bcast[:, :], pX[:, 0:16]))
            fv(v.tensor_tensor(t_c1[:, :], s_2[:, :], s_up[:, :], ALU.mult))
            fv(v.tensor_tensor(k_o[:, :], bcast[:, 0:1], t_c1[:, :], ALU.mult))
            fv(v.tensor_scalar(s_o[:, :], k_o[:, :], 1.0 / QMAX, 1e-12, ALU.mult, ALU.max))
            fv(v.reciprocal(k_o[:, :], s_o[:, :]))
            sBC.inc(v.tensor_tensor(k_o[:, :], k_o[:, :], t_c1[:, :], ALU.mult))
        BC4 = sBC.n

        if stage == "f4":
            early_out([(sBC, BC4)])
            es_pc.close()
            es_cw.close()
            es_c2.close()
            return nc

        _scope_end()
        _scope_begin("ffn5a")
        SGA_5A0 = sSGA.n

        def m5(m):
            ts_, ic = m // ICH, m % ICH
            return ic * NTS + ts_

        @blk.sync
        def _(sync):
            sync.wait_ge(sBC.h, BC4)
            for m in range(NT6):
                ts_, ic = m // ICH, m % ICH
                if m >= 2:
                    sync.wait_ge(sSGB.h, 2 * NT6 + m - 1)
                (sUP50 if m % 2 == 0 else sUP51).inc(sync.dma_start(
                    out=utl[m % 2][:, :],
                    in_=updram[ic * 128:(ic + 1) * 128, ts_ * 512:ts_ * 512 + 512]), 16)

        @blk.scalar
        def _(s):
            s.wait_ge(sBC.h, BC4)
            for m in range(NT6 + 1):
                if m < NT6:
                    s.wait_ge((sUP50 if m % 2 == 0 else sUP51).h, 16 * (m // 2 + 1))
                    if m >= 2:
                        s.wait_ge(sOC.h, RM_O_OC + m - 1)
                    sSGB.inc(s.activation(ui16[m % 2][:, :], utl[m % 2][:, :], AF.Identity,
                                          scale=r_up[:, :]))
                if m >= 1:
                    s.wait_ge(sOC.h, RM_O_OC + m)
                    sSGA.inc(s.activation(oi32b[(m - 1) % 2][:, :], g2t2[(m - 1) % 2][:, :],
                                          AF.Identity, scale=k_o[:, :]))

        @blk.vector
        def _(v):
            v.wait_ge(sBC.h, BC4)
            for m in range(NT6 + 1):
                if m < NT6:
                    v.wait_ge(sSGB.h, 2 * NT6 + m + 1)
                    if m >= 2:
                        v.wait_ge(sSGA.h, SGA_5A0 + m - 1)
                    sOC.inc(v.tensor_tensor(g2t2[m % 2][:, :], gi_sl(m5(m)),
                                            ui16[m % 2][:, :], ALU.mult))
                if m >= 1:
                    v.wait_ge(sSGA.h, SGA_5A0 + m)
                    sQ2s.inc(v.tensor_scalar(gi_sl(m5(m - 1), BF16),
                                             oi32b[(m - 1) % 2][:, :],
                                             s_o[:, :], None, ALU.mult))
        Q2S_ALL = sQ2s.n

        if stage == "f5":
            early_out([(sQ2s, NT6)])
            es_pc.close()
            es_cw.close()
            es_c2.close()
            return nc

        _scope_end()
        _scope_begin("ffn5b")
        @blk.sync
        def _(sync):
            for it in range(NCH + 1):
                if it < NCH:
                    if it >= 2 and stage != "f7":
                        sync.wait_ge(sY2.h, it - 1)
                    (sX1L if it % 2 == 0 else sX1L1).inc(
                        sync.dma_start(out=x1t[it % 2][:, :],
                                       in_=x1dram[:, it * D:(it + 1) * D]), 16)
                if it >= 1 and stage not in ("f7", "f8", "f9"):
                    sync.wait_ge(sLN2.h, it)
                    sOUT.inc(sync.dma_start(out=out_h[(it - 1) * S:it * S, :],
                                            in_=outt[:, :]), 16)
            if stage not in ("f7", "f8", "f9"):
                sync.wait_ge(sOUT.h, sOUT.n)

        @blk.tensor
        def _(t):
            t.wait_ge(sWD.h, WD_ALL)
            for tc in range(NCH):
                t.wait_ge(sQ2s.h, ICH * (tc // 4 + 1))
                if tc >= 2 and stage != "f7":
                    t.wait_ge(sY2.h, tc - 1)
                pp = pX if tc % 2 == 0 else pXb
                for half in range(2):
                    for ic in range(ICH):
                        ins = t.matmul(
                            pp[:, half * 512:(half + 1) * 512],
                            GI[:, ic * T + tc * 128: ic * T + (tc + 1) * 128].bitcast(BF16),
                            wdn[:, ic * 1024 + half * 512: ic * 1024 + half * 512 + 512],
                            start=(ic == 0), stop=(ic == ICH - 1))
                sDM.inc(ins)

        if stage == "f8":
            pass
        if stage == "f7":
            early_out([(sDM, NCH), (sX1L, 16 * (NCH // 2)), (sX1L1, 16 * (NCH // 2))])
            es_pc.close()
            es_cw.close()
            es_c2.close()
            return nc

        @blk.vector
        def _(v):
            for tc in range(NCH):
                v.wait_ge(sDM.h, tc + 1)
                v.wait_ge((sX1L if tc % 2 == 0 else sX1L1).h, 16 * (tc // 2 + 1))
                pp = pX if tc % 2 == 0 else pXb
                sY2.inc(v.scalar_tensor_tensor(y2[:, :], x1t[tc % 2][:, :], 1.0, pp[:, :],
                                               ALU.mult, ALU.add, accum_out=sums[:, :]))
                if stage == "f8":
                    continue
                if stage in ("f6", "f9", "f10"):
                    if tc >= 1 and stage in ("f6", "f10"):
                        v.wait_ge(sOUT.h, 16 * tc)
                    sLN2.inc(v.tensor_copy(outt[:, :], y2[:, :]))
                    continue
                v.wait_ge(sSQ.h, NCH + tc + 1)
                fv = mkfence(v, sFV)
                v.tensor_scalar(mean[:, :], sums[:, :], 1.0 / D, None, ALU.mult)
                fv(v.tensor_scalar(ex2[:, :], sumsq[:, :], 1.0 / D, None, ALU.mult))
                fv(v.tensor_tensor(t_c1[:, :], mean[:, :], mean[:, :], ALU.mult))
                fv(v.tensor_tensor(varb[:, :], ex2[:, :], t_c1[:, :], ALU.subtract))
                sVR.inc(v.tensor_scalar(varb[:, :], varb[:, :], 1e-5, None, ALU.add))
                v.wait_ge(sSD.h, NCH + tc + 1)
                fv(v.reciprocal(rstd[:, :], sdb[:, :]))
                if tc >= 1:
                    v.wait_ge(sOUT.h, 16 * tc)
                sLN2.inc(v.tensor_scalar(outt[:, :], y2[:, :], mean[:, :], rstd[:, :],
                                         ALU.subtract, ALU.mult))

        if stage == "f10":
            es_pc.close()
            es_cw.close()
            es_c2.close()
            return nc

        if stage == "f9":
            early_out([(sLN2, NCH)])
            es_pc.close()
            es_cw.close()
            es_c2.close()
            return nc

        if stage == "f8":
            early_out([(sY2, NCH)])
            es_pc.close()
            es_cw.close()
            es_c2.close()
            return nc

        if stage not in ("f6", "f9", "f10"):
            @blk.scalar
            def _(s):
                for tc in range(NCH):
                    s.wait_ge(sY2.h, tc + 1)
                    sSQ.inc(s.activation(y2b[:, :], y2[:, :], AF.Square, accum_out=sumsq[:, :]))
                    s.wait_ge(sVR.h, NCH + tc + 1)
                    sSD.inc(s.activation(sdb[:, :], varb[:, :], AF.Sqrt))

        _scope_end()
        es_pc.close()
        es_cw.close()
        es_c2.close()

    return nc


_NC_CACHE = {}


def _get_nc(stage):
    if stage not in _NC_CACHE:
        _NC_CACHE[stage] = build(stage)
    return _NC_CACHE[stage]


def kernel(L0_attn_inputs, L0_attn_nu_log, L0_attn_theta_log, L0_attn_B_re, L0_attn_B_im,
           L0_attn_C_re, L0_attn_C_im, L0_attn_D_skip, L0_ffn_w_gate, L0_ffn_w_up,
           L0_ffn_w_down):
    x = np.ascontiguousarray(np.asarray(L0_attn_inputs, dtype=np.float32))
    nu_log = np.asarray(L0_attn_nu_log, dtype=np.float32)
    th_log = np.asarray(L0_attn_theta_log, dtype=np.float32)
    B_re = np.asarray(L0_attn_B_re, dtype=np.float32)
    B_im = np.asarray(L0_attn_B_im, dtype=np.float32)
    C_re = np.asarray(L0_attn_C_re, dtype=np.float32)
    C_im = np.asarray(L0_attn_C_im, dtype=np.float32)
    D_skip = np.asarray(L0_attn_D_skip, dtype=np.float32)
    w_gate = np.asarray(L0_ffn_w_gate, dtype=np.float32)
    w_up = np.asarray(L0_ffn_w_up, dtype=np.float32)
    w_down = np.asarray(L0_ffn_w_down, dtype=np.float32)

    btre = np.ascontiguousarray(B_re.T)
    btim = np.ascontiguousarray(B_im.T)
    ctre = np.ascontiguousarray(C_re.T)
    ctim = np.ascontiguousarray(C_im.T)
    nucol = np.ascontiguousarray(nu_log.reshape(8, 128).T)
    thcol = np.ascontiguousarray(th_log.reshape(8, 128).T)
    nurow = np.ascontiguousarray(nu_log.reshape(1, H))
    throw = np.ascontiguousarray(th_log.reshape(1, H))
    dskrow = np.ascontiguousarray(D_skip.reshape(1, D))
    identf = np.eye(128, dtype=np.float32)
    identb = np.eye(128, dtype=ml_dtypes.bfloat16)
    ltones = np.triu(np.ones((128, 128), dtype=np.float64)).astype(ml_dtypes.bfloat16)
    iotac = np.arange(128, dtype=np.float32).reshape(128, 1)
    iotacat = np.ascontiguousarray(
        np.broadcast_to(np.tile(np.arange(128, dtype=np.float32), 8), (128, 1024)))
    onesr = np.ones((1, 128), dtype=np.float32)

    nc = _get_nc(_STAGE)
    in_maps = []
    for c in range(8):
        in_maps.append(dict(
            x=np.ascontiguousarray(x[c]),
            btre=btre, btim=btim, ctre=ctre, ctim=ctim,
            nucol=nucol, thcol=thcol, nurow=nurow, throw=throw, dskrow=dskrow,
            wups=np.ascontiguousarray(w_up[c * ISL:(c + 1) * ISL, :]),
            wgts=np.ascontiguousarray(w_gate[c * ISL:(c + 1) * ISL, :]),
            wdns=np.ascontiguousarray(w_down[:, c * ISL:(c + 1) * ISL]),
            identf=identf, identb=identb, ltones=ltones,
            iotac=iotac, iotacat=iotacat, onesr=onesr,
        ))
    res = run_bass_kernel_spmd(nc, in_maps, core_ids=list(range(8)))
    return np.stack([np.asarray(r["out"]) for r in res.results], axis=0)
